# version 01
# baseline (speedup 1.0000x reference)
import numpy as np
import concourse.bass as bass
import concourse.mybir as mybir
from concourse.bass_utils import run_bass_kernel_spmd

F32 = mybir.dt.float32
BF16 = mybir.dt.bfloat16
I8 = mybir.dt.int8
I32 = mybir.dt.int32
AF = mybir.ActivationFunctionType
ALU = mybir.AluOpType
AX = mybir.AxisListType

NCORE = 8
L = 8192
D = 2048
DT = D // 128
HK, HV = 16, 32
DK = DV = 128
REP = HV // HK
KEY_DIM = HK * DK
VAL_DIM = HV * DV
EPS = 1e-6
LN_EPS = 1e-5
NEG = -1e30

TS = 512
NSLAB = L // TS
C = 128
CPS = TS // C

ENGINES = ("sync", "gpsimd", "tensor", "vector", "scalar")


class Sched:

    def __init__(self, nc):
        self.nc = nc
        self.tasks = {e: [] for e in ENGINES}
        self.last_writer = {}
        self.readers = {}
        self.sems = {}
        self.dsems = {}
        self.dcount = {e: 0 for e in ENGINES}
        self.mcount = {e: 0 for e in ENGINES}
        self.bcount = {e: 0 for e in ENGINES}
        self.bsems = {}
        self._ctx = []
        for e in ENGINES:
            cm = nc.semaphore(f"ms_{e}")
            self.sems[e] = cm.__enter__()
            self._ctx.append(cm)
            cd = nc.semaphore(f"ds_{e}")
            self.dsems[e] = cd.__enter__()
            self._ctx.append(cd)
            cb = nc.semaphore(f"bs_{e}")
            self.bsems[e] = cb.__enter__()
            self._ctx.append(cb)

    def task(self, engine, fn, reads=(), writes=(), dma=0, tag=""):
        deps = {}

        def add(tok):
            if tok is None:
                return
            e, v = tok
            deps[e] = max(deps.get(e, 0), v)

        for nm in reads:
            add(self.last_writer.get(nm))
        for nm in writes:
            add(self.last_writer.get(nm))
            for t in self.readers.get(nm, ()):
                add(t)
        self.mcount[engine] += 1
        tok = (engine, self.mcount[engine])
        waits = [(self.sems[e], v) for e, v in sorted(deps.items())]
        if dma:
            self.dcount[engine] += 16 * dma
        self.tasks[engine].append((fn, waits, dma, self.dcount[engine], tag))
        for nm in reads:
            self.readers.setdefault(nm, []).append(tok)
        for nm in writes:
            self.last_writer[nm] = tok
            self.readers[nm] = []
        return tok

    def bar(self, eng):
        e = self._cur
        self.bcount[e] += 1
        eng.engine_nop().then_inc(self.bsems[e], 1)
        eng.wait_ge(self.bsems[e], self.bcount[e])

    def run(self, engine, eng):
        self._cur = engine
        msem = self.sems[engine]
        dsem = self.dsems[engine]
        for fn, waits, dma, dtarget, tag in self.tasks[engine]:
            for sem, v in waits:
                eng.wait_ge(sem, v)
            if dma:
                n = fn(eng, dsem)
                assert n == dma, f"task {tag}: declared {dma} DMAs, emitted {n}"
                eng.wait_ge(dsem, dtarget)
                eng.sem_inc(msem, 1)
            elif engine == "vector":
                fn(eng)
                eng.engine_nop().then_inc(msem, 1)
            else:
                last = fn(eng)
                assert last is not None, f"task {tag} returned no instruction"
                last.then_inc(msem, 1)


def _f32(x):
    return np.ascontiguousarray(x, dtype=np.float32)


def host_consts_A():
    ident = np.eye(128, dtype=np.float32)
    cums = np.triu(np.ones((128, 128), np.float32))
    maskA = np.where(np.tril(np.ones((128, 128)), -1) > 0, 0.0, NEG).astype(np.float32)
    maskAT = np.where(np.triu(np.ones((128, 128)), 1) > 0, 0.0, NEG).astype(np.float32)
    maskPT = np.where(np.triu(np.ones((128, 128)), 0) > 0, 0.0, NEG).astype(np.float32)
    masks4 = np.concatenate([np.tile(m, (1, 4)) for m in (maskA, maskAT, maskPT)], axis=1)
    sel4 = np.zeros((4, 512), np.float32)
    for h in range(4):
        sel4[h, h * 128:(h + 1) * 128] = 1.0
    oh4 = np.zeros((128, 8), np.float32)
    oh4[:, 0] = 1.0
    oh4[:, 1] = 1.0
    oh4[:, 6] = 1.0
    oh4[:, 7] = 1.0
    return ident, cums, masks4, sel4, oh4


def stage_launch_A(inp):
    xT = _f32(np.asarray(inp["L0_attn_hidden_states"])[0].T)
    Wq = np.asarray(inp["L0_attn_W_qkvz"])
    Wb = np.asarray(inp["L0_attn_W_ba"])
    convw = np.asarray(inp["L0_attn_conv_w"])
    dtb = np.asarray(inp["L0_attn_dt_bias"])
    alog = np.asarray(inp["L0_attn_A_log"])
    gnw = np.asarray(inp["L0_attn_gated_norm_w"])
    ln1 = np.asarray(inp["L0_attn_ln1_w"])
    Wq4 = Wq.reshape(D, HK, 2 * DK + 2 * REP * DV)
    Wb4 = Wb.reshape(D, HK, 2 * REP)
    ident, cums, masks4, sel4, oh4 = host_consts_A()
    per_core = []
    for c in range(NCORE):
        hq = [2 * c, 2 * c + 1]
        hv = [4 * c + i for i in range(4)]
        wmixz = np.concatenate(
            [Wq4[:, h, 0:DK] for h in hq]
            + [Wq4[:, h, DK:2 * DK] for h in hq]
            + [Wq4[:, h, 2 * DK:2 * DK + REP * DV] for h in hq]
            + [Wq4[:, h, 2 * DK + REP * DV:] for h in hq], axis=1)
        wba = np.concatenate([Wb4[:, h, 0:REP] for h in hq]
                             + [Wb4[:, h, REP:] for h in hq], axis=1)
        rows = (list(range(2 * c * DK, (2 * c + 2) * DK))
                + list(range(KEY_DIM + 2 * c * DK, KEY_DIM + (2 * c + 2) * DK))
                + list(range(2 * KEY_DIM + 4 * c * DV, 2 * KEY_DIM + (4 * c + 4) * DV)))
        per_core.append({
            "xT": xT,
            "wmixz": _f32(wmixz),
            "wba": _f32(wba),
            "convc": _f32(convw[rows]),
            "dtb": _f32(dtb[hv].reshape(4, 1)),
            "alog": _f32(alog[hv].reshape(4, 1)),
            "gnw": _f32(gnw.reshape(1, 128)),
            "ln1w": _f32(ln1.reshape(DT, 128).T),
            "ident": ident, "cums": cums, "masks4": masks4, "sel4": sel4, "oh4": oh4,
        })
    return per_core


def build_launch_A():
    nc = bass.Bass("TRN2", target_bir_lowering=False)
    xT_t = nc.dram_tensor("xT", [D, L], F32, kind="ExternalInput").ap()
    wmixz_t = nc.dram_tensor("wmixz", [D, 1536], F32, kind="ExternalInput").ap()
    wba_t = nc.dram_tensor("wba", [D, 8], F32, kind="ExternalInput").ap()
    convc_t = nc.dram_tensor("convc", [1024, 4], F32, kind="ExternalInput").ap()
    dtb_t = nc.dram_tensor("dtb", [4, 1], F32, kind="ExternalInput").ap()
    alog_t = nc.dram_tensor("alog", [4, 1], F32, kind="ExternalInput").ap()
    gnw_t = nc.dram_tensor("gnw", [1, 128], F32, kind="ExternalInput").ap()
    ln1w_t = nc.dram_tensor("ln1w", [128, DT], F32, kind="ExternalInput").ap()
    ident_t = nc.dram_tensor("ident", [128, 128], F32, kind="ExternalInput").ap()
    cums_t = nc.dram_tensor("cums", [128, 128], F32, kind="ExternalInput").ap()
    masks4_t = nc.dram_tensor("masks4", [128, 3 * 512], F32, kind="ExternalInput").ap()
    sel4_t = nc.dram_tensor("sel4", [4, 512], F32, kind="ExternalInput").ap()
    oh4_t = nc.dram_tensor("oh4", [128, 8], F32, kind="ExternalInput").ap()
    gatedT_t = nc.dram_tensor("gatedT", [512, L], F32, kind="ExternalOutput").ap()

    NT_MIX, NT_Z = 8, 4
    MW = TS + 3
    LOG_SCALE_Q = float(np.log(DK ** -0.5))

    from contextlib import ExitStack
    with ExitStack() as st:
        w_sb = st.enter_context(nc.sbuf_tensor("w_sb", [128, DT * 1536], BF16))
        wba_sb = st.enter_context(nc.sbuf_tensor("wba_sb", [128, DT * 8], BF16))
        convw_sb = st.enter_context(nc.sbuf_tensor("convw_sb", [128, 8 * 4], F32))
        ln1_sb = st.enter_context(nc.sbuf_tensor("ln1_sb", [128, DT], F32))
        gnwb = st.enter_context(nc.sbuf_tensor("gnwb", [128, 512], BF16))
        ident_b = st.enter_context(nc.sbuf_tensor("ident_b", [128, 128], BF16))
        ident_f = st.enter_context(nc.sbuf_tensor("ident_f", [128, 128], F32))
        cums_f = st.enter_context(nc.sbuf_tensor("cums_f", [128, 128], F32))
        masks4_f = st.enter_context(nc.sbuf_tensor("masks4_f", [128, 3 * 512], F32))
        sel4_f = st.enter_context(nc.sbuf_tensor("sel4_f", [4, 512], F32))
        oh4_b = st.enter_context(nc.sbuf_tensor("oh4_b", [128, 8], BF16))
        ones1f = st.enter_context(nc.sbuf_tensor("ones1f", [1, 128], F32))
        ones1b = st.enter_context(nc.sbuf_tensor("ones1b", [1, 128], BF16))
        onescol_b = st.enter_context(nc.sbuf_tensor("onescol_b", [128, 1], BF16))
        consts = st.enter_context(nc.sbuf_tensor("consts", [128, 4], F32))
        dtb_sb = st.enter_context(nc.sbuf_tensor("dtb_sb", [4, 1], F32))
        alog_sb = st.enter_context(nc.sbuf_tensor("alog_sb", [4, 1], F32))
        nega_sb = st.enter_context(nc.sbuf_tensor("nega", [4, 1], F32))
        gnw_sb = st.enter_context(nc.sbuf_tensor("gnw_sb", [1, 128], F32))
        xslab = st.enter_context(nc.sbuf_tensor("xslab", [128, DT * TS], BF16))
        xsq = st.enter_context(nc.sbuf_tensor("xsq", [128, DT * TS], BF16))
        hslab_b = st.enter_context(nc.sbuf_tensor("hslab_b", [128, DT * TS], BF16))
        stats = st.enter_context(nc.sbuf_tensor("stats", [1, 8 * TS], F32))
        mixed = st.enter_context(nc.sbuf_tensor("mixed", [128, NT_MIX * MW], BF16))
        convo = st.enter_context(nc.sbuf_tensor("convo", [128, NT_MIX * TS], BF16))
        convacc = st.enter_context(nc.sbuf_tensor("convacc", [128, TS], F32))
        htmp = st.enter_context(nc.sbuf_tensor("htmp", [128, TS], F32))
        zsil = st.enter_context(nc.sbuf_tensor("zsil", [128, NT_Z * TS], BF16))
        braw = st.enter_context(nc.sbuf_tensor("braw", [4, TS], F32))
        araw = st.enter_context(nc.sbuf_tensor("araw", [4, TS], F32))
        spbuf = st.enter_context(nc.sbuf_tensor("spbuf", [4, TS], F32))
        g4 = st.enter_context(nc.sbuf_tensor("g4", [4, TS], F32))
        lnb4 = st.enter_context(nc.sbuf_tensor("lnb4", [4, TS], F32))
        beta4 = st.enter_context(nc.sbuf_tensor("beta4", [4, TS], F32))
        uq4 = st.enter_context(nc.sbuf_tensor("uq4", [4, TS], F32))
        uk4 = st.enter_context(nc.sbuf_tensor("uk4", [4, TS], F32))
        sqbuf = st.enter_context(nc.sbuf_tensor("sqbuf", [128, 4 * TS], BF16))
        bfeat = st.enter_context(nc.sbuf_tensor("bfeat", [4, 128], F32))
        cfA = st.enter_context(nc.sbuf_tensor("cfA", [4, 128], F32))
        rfA = st.enter_context(nc.sbuf_tensor("rfA", [4, 128], F32))
        rfP = st.enter_context(nc.sbuf_tensor("rfP", [4, 128], F32))
        rfacf = st.enter_context(nc.sbuf_tensor("rfacf", [4, 128], F32))
        pfacf = st.enter_context(nc.sbuf_tensor("pfacf", [4, 128], F32))
        gfacf = st.enter_context(nc.sbuf_tensor("gfacf", [4, 128], F32))
        ebcf = st.enter_context(nc.sbuf_tensor("ebcf", [4, 1], F32))
        cvec = st.enter_context(nc.sbuf_tensor("cvec", [128, 32], F32))
        argbuf = st.enter_context(nc.sbuf_tensor("argbuf", [128, 512], F32))
        kgb = st.enter_context(nc.sbuf_tensor("kgb", [128, 256], BF16))
        qkgb = st.enter_context(nc.sbuf_tensor("qkgb", [128, 256], BF16))
        Abf = st.enter_context(nc.sbuf_tensor("Abf", [128, 512], BF16))
        ATbf = st.enter_context(nc.sbuf_tensor("ATbf", [128, 512], BF16))
        SqA = st.enter_context(nc.sbuf_tensor("SqA", [128, 512], BF16))
        SqAT = st.enter_context(nc.sbuf_tensor("SqAT", [128, 512], BF16))
        SqB = st.enter_context(nc.sbuf_tensor("SqB", [128, 512], BF16))
        SqBT = st.enter_context(nc.sbuf_tensor("SqBT", [128, 512], BF16))
        PTm = st.enter_context(nc.sbuf_tensor("PTm", [128, 512], BF16))
        Vtok = st.enter_context(nc.sbuf_tensor("Vtok", [128, 512], BF16))
        Vr = st.enter_context(nc.sbuf_tensor("Vr", [128, 512], BF16))
        Ubf = st.enter_context(nc.sbuf_tensor("Ubf", [128, 512], BF16))
        Pm = st.enter_context(nc.sbuf_tensor("Pm", [128, 512], BF16))
        ktokb = st.enter_context(nc.sbuf_tensor("ktokb", [128, 256], BF16))
        kbar = st.enter_context(nc.sbuf_tensor("kbar", [128, 512], BF16))
        otok = st.enter_context(nc.sbuf_tensor("otok", [128, 512], F32))
        ztok = st.enter_context(nc.sbuf_tensor("ztok", [128, 512], BF16))
        gtmp = st.enter_context(nc.sbuf_tensor("gtmp", [128, 512], F32))
        gatedtok = st.enter_context(nc.sbuf_tensor("gatedtok", [128, 512], BF16))
        gatedf = st.enter_context(nc.sbuf_tensor("gatedf", [128, 4 * TS], F32))
        ssum = st.enter_context(nc.sbuf_tensor("ssum", [128, 4], F32))
        rstd = st.enter_context(nc.sbuf_tensor("rstd", [128, 4], F32))
        S_sb = st.enter_context(nc.sbuf_tensor("S", [128, 512], F32))
        S_bf = st.enter_context(nc.sbuf_tensor("S_bf", [128, 512], BF16))
        P0 = st.enter_context(nc.psum_tensor("P0", [128, 512], F32))
        P1 = st.enter_context(nc.psum_tensor("P1", [128, 512], F32))
        P2 = st.enter_context(nc.psum_tensor("P2", [128, 512], F32))
        P3 = st.enter_context(nc.psum_tensor("P3", [128, 512], F32))
        P4 = st.enter_context(nc.psum_tensor("P4", [128, 512], F32))
        P5 = st.enter_context(nc.psum_tensor("P5", [128, 512], F32))
        P6 = st.enter_context(nc.psum_tensor("P6", [128, 512], F32))
        PB7 = st.enter_context(nc.psum_tensor("PB7", [128, 1024], BF16))
        block = st.enter_context(nc.Block())
        sch = Sched(nc)
        PS = [P0, P1, P2, P3, P4, P5, P6]
        psn = [f"P{i}" for i in range(7)]

        def t_pre_sync(eng, dsem):
            n = 0
            for dst, src in ((ident_f[:], ident_t), (cums_f[:], cums_t),
                             (masks4_f[:], masks4_t), (ln1_sb[:], ln1w_t),
                             (dtb_sb[:], dtb_t), (gnw_sb[:], gnw_t),
                             (sel4_f[:], sel4_t), (alog_sb[:], alog_t)):
                eng.dma_start(out=dst, in_=src).then_inc(dsem, 16)
                n += 1
            for i in range(8):
                eng.dma_start(out=convw_sb[:, 4 * i:4 * i + 4],
                              in_=convc_t[128 * i:128 * (i + 1), :]).then_inc(dsem, 16)
                n += 1
            return n
        sch.task("sync", t_pre_sync,
                 writes=["ident_f", "cums_f", "masks4_f", "ln1_sb", "dtb_sb",
                         "gnw_sb", "sel4_f", "alog_sb", "convw_sb"],
                 dma=16, tag="preS")

        def t_pre_gp(eng, dsem):
            eng.dma_start(out=w_sb[:].rearrange("p (t n) -> p t n", t=DT), in_=wmixz_t.rearrange("(t p) n -> p t n", p=128)).then_inc(dsem, 16)
            eng.dma_start(out=wba_sb[:].rearrange("p (t n) -> p t n", t=DT), in_=wba_t.rearrange("(t p) n -> p t n", p=128)).then_inc(dsem, 16)
            eng.dma_start(out=oh4_b[:], in_=oh4_t).then_inc(dsem, 16)
            return 3
        sch.task("gpsimd", t_pre_gp, writes=["w_sb", "wba_sb", "oh4_b"], dma=3, tag="preG")

        def t_pre_vec(eng):
            eng.memset(ones1f[:], 1.0)
            eng.memset(ones1b[:], 1.0)
            eng.memset(onescol_b[:], 1.0)
            eng.memset(consts[:, 0:1], EPS)
            eng.memset(consts[:, 1:2], LN_EPS)
            eng.memset(consts[:, 2:3], 0.0)
            eng.memset(consts[:, 3:4], 0.0)
            eng.memset(S_sb[:], 0.0)
            eng.memset(S_bf[:], 0.0)
            for nt in range(NT_MIX):
                eng.memset(mixed[:, nt * MW:nt * MW + 3], 0.0)
            return eng.tensor_copy(out=ident_b[:], in_=ident_f[:])
        sch.task("vector", t_pre_vec, reads=["ident_f"],
                 writes=["ones1f", "ones1b", "onescol_b", "consts", "S", "S_bf", "ident_b"]
                        + [f"mixed{nt}" for nt in range(NT_MIX)],
                 tag="preV")

        def t_fold_ln1(eng):
            for dt in range(DT):
                eng.tensor_scalar_mul(out=w_sb[:, dt * 1536:(dt + 1) * 1536],
                                      in0=w_sb[:, dt * 1536:(dt + 1) * 1536],
                                      scalar1=ln1_sb[:, dt:dt + 1])
            for dt in range(DT):
                eng.tensor_scalar_mul(out=wba_sb[:, dt * 8:(dt + 1) * 8],
                                      in0=wba_sb[:, dt * 8:(dt + 1) * 8],
                                      scalar1=ln1_sb[:, dt:dt + 1])
            return eng.tensor_copy(out=oh4_b[:], in_=oh4_b[:])
        sch.task("vector", t_fold_ln1, reads=["ln1_sb", "w_sb", "wba_sb", "oh4_b"],
                 writes=["w_sb", "wba_sb"], tag="foldln1")

        def t_nega(eng):
            eng.activation(out=nega_sb[:], in_=alog_sb[:], func=AF.Exp)
            return eng.mul(out=nega_sb[:], in_=nega_sb[:], mul=-1.0)
        sch.task("scalar", t_nega, reads=["alog_sb"], writes=["nega"], tag="nega")

        def t_gnwb_mm(eng):
            for h in range(4):
                i = eng.matmul(P6[:, h * 128:(h + 1) * 128], ones1f[0:1, :], gnw_sb[0:1, :],
                               start=True, stop=True)
            return i
        sch.task("tensor", t_gnwb_mm, reads=["ones1f", "gnw_sb"], writes=["P6"], tag="gnwmm")

        def t_gnwb_ev(eng):
            return eng.tensor_copy(out=gnwb[:], in_=P6[:])
        sch.task("vector", t_gnwb_ev, reads=["P6"], writes=["gnwb"], tag="gnwev")

        for s in range(NSLAB):
            t0 = s * TS
            sfx = f"s{s}"

            sch.task("gpsimd",
                     lambda eng, dsem, t0=t0: (eng.dma_start(
                         out=xslab[:].rearrange("p (t n) -> p t n", t=DT),
                         in_=xT_t[:, t0:t0 + TS].rearrange("(t p) n -> p t n", p=128)
                     ).then_inc(dsem, 16), 1)[1],
                     writes=["xslab"], dma=1, tag="ldx" + sfx)

            sch.task("scalar",
                     lambda eng: eng.activation(out=xsq[:], in_=xslab[:], func=AF.Square),
                     reads=["xslab"], writes=["xsq"], tag="sq" + sfx)

            def t_mm_stats(eng):
                for dt in range(DT):
                    eng.matmul(P4[0:1, 0:TS], onescol_b[:], xslab[:, dt * TS:(dt + 1) * TS],
                               start=(dt == 0), stop=(dt == DT - 1))
                for dt in range(DT):
                    i = eng.matmul(P5[0:1, 0:TS], onescol_b[:], xsq[:, dt * TS:(dt + 1) * TS],
                                   start=(dt == 0), stop=(dt == DT - 1))
                return i
            sch.task("tensor", t_mm_stats, reads=["onescol_b", "xslab", "xsq"],
                     writes=["P4", "P5"], tag="mmstat" + sfx)

            T1 = TS
            def t_stat1(eng):
                eng.tensor_scalar_mul(out=stats[0:1, 0:T1], in0=P4[0:1, 0:TS], scalar1=1.0 / D)
                eng.tensor_scalar_mul(out=stats[0:1, T1:2 * T1], in0=P5[0:1, 0:TS], scalar1=1.0 / D)
                eng.tensor_mul(out=stats[0:1, 4 * T1:5 * T1], in0=stats[0:1, 0:T1],
                               in1=stats[0:1, 0:T1])
                return eng.tensor_sub(out=stats[0:1, T1:2 * T1], in0=stats[0:1, T1:2 * T1],
                                      in1=stats[0:1, 4 * T1:5 * T1])
            sch.task("vector", t_stat1, reads=["P4", "P5"], writes=["stats"], tag="st1" + sfx)

            def t_stat2(eng):
                eng.activation(out=stats[0:1, 4 * T1:5 * T1], in_=stats[0:1, T1:2 * T1],
                               func=AF.Ln, bias=consts[0:1, 1:2])
                return eng.activation(out=stats[0:1, 4 * T1:5 * T1],
                                      in_=stats[0:1, 4 * T1:5 * T1], func=AF.Exp, scale=-0.5)
            sch.task("scalar", t_stat2, reads=["stats", "consts"], writes=["stats"], tag="st2" + sfx)

            def t_stat3(eng):
                eng.tensor_mul(out=stats[0:1, 5 * T1:6 * T1], in0=stats[0:1, 4 * T1:5 * T1],
                               in1=stats[0:1, 4 * T1:5 * T1])
                return eng.tensor_mul(out=stats[0:1, 5 * T1:6 * T1],
                                      in0=stats[0:1, 5 * T1:6 * T1], in1=stats[0:1, T1:2 * T1])
            sch.task("vector", t_stat3, reads=["stats"], writes=["stats"], tag="st3" + sfx)

            def t_stat4(eng):
                eng.activation(out=stats[0:1, 5 * T1:6 * T1], in_=stats[0:1, 5 * T1:6 * T1],
                               func=AF.Ln, bias=consts[0:1, 0:1])
                return eng.activation(out=stats[0:1, 5 * T1:6 * T1],
                                      in_=stats[0:1, 5 * T1:6 * T1], func=AF.Exp, scale=-0.5)
            sch.task("scalar", t_stat4, reads=["stats", "consts"], writes=["stats"], tag="st4" + sfx)

            def t_stat5(eng):
                eng.tensor_mul(out=stats[0:1, 2 * T1:3 * T1], in0=stats[0:1, 4 * T1:5 * T1],
                               in1=stats[0:1, 5 * T1:6 * T1])
                eng.tensor_mul(out=stats[0:1, 3 * T1:4 * T1], in0=stats[0:1, 0:T1],
                               in1=stats[0:1, 2 * T1:3 * T1])
                return eng.tensor_scalar_mul(out=stats[0:1, 3 * T1:4 * T1],
                                             in0=stats[0:1, 3 * T1:4 * T1], scalar1=-1.0)
            sch.task("vector", t_stat5, reads=["stats"], writes=["stats"], tag="st5" + sfx)

            def t_bcast_pq(eng):
                eng.matmul(P4[:, 0:TS], ones1f[0:1, :], stats[0:1, 2 * T1:3 * T1],
                           start=True, stop=True)
                return eng.matmul(P5[:, 0:TS], ones1f[0:1, :], stats[0:1, 3 * T1:4 * T1],
                                  start=True, stop=True)
            sch.task("tensor", t_bcast_pq, reads=["ones1f", "stats"], writes=["P4", "P5"],
                     tag="bcpq" + sfx)

            def t_mk_h(eng):
                for dt in range(DT):
                    sl = slice(dt * TS, (dt + 1) * TS)
                    eng.tensor_mul(out=htmp[:], in0=xslab[:, sl], in1=P4[:, 0:TS])
                    i = eng.tensor_add(out=hslab_b[:, sl], in0=htmp[:], in1=P5[:, 0:TS])
                return i
            sch.task("vector", t_mk_h, reads=["xslab", "P4", "P5"],
                     writes=["htmp", "hslab_b"], tag="mkh" + sfx)

            for nt in range(12):
                pb, pn = PS[nt % 4], psn[nt % 4]
                def t_proj(eng, nt=nt, pb=pb):
                    for dt in range(DT):
                        i = eng.matmul(pb[:, 0:TS],
                                       w_sb[:, dt * 1536 + nt * 128: dt * 1536 + (nt + 1) * 128],
                                       hslab_b[:, dt * TS:(dt + 1) * TS],
                                       start=(dt == 0), stop=(dt == DT - 1))
                    return i
                sch.task("tensor", t_proj, reads=["w_sb", "hslab_b"], writes=[pn],
                         tag=f"prj{nt}" + sfx)
                if nt < NT_MIX:
                    def t_evm(eng, nt=nt, pb=pb, s=s):
                        base = nt * MW
                        if s > 0:
                            eng.tensor_copy(out=mixed[:, base:base + 3],
                                            in_=mixed[:, base + TS:base + TS + 3])
                        return eng.tensor_copy(out=mixed[:, base + 3:base + 3 + TS],
                                               in_=pb[:, 0:TS])
                    sch.task("vector", t_evm, reads=[pn, f"mixed{nt}"],
                             writes=[f"mixed{nt}"], tag=f"evm{nt}" + sfx)
                else:
                    zt = nt - NT_MIX
                    def t_evz(eng, zt=zt, pb=pb):
                        return eng.tensor_copy(out=zsil[:, zt * TS:(zt + 1) * TS], in_=pb[:, 0:TS])
                    sch.task("vector", t_evz, reads=[pn], writes=[f"zsil{zt}"],
                             tag=f"evz{zt}" + sfx)

            def t_proj_b(eng):
                for dt in range(DT):
                    eng.matmul(P4[0:4, 0:TS], wba_sb[:, dt * 8:dt * 8 + 4],
                               hslab_b[:, dt * TS:(dt + 1) * TS],
                               start=(dt == 0), stop=(dt == DT - 1))
                for dt in range(DT):
                    i = eng.matmul(P5[0:4, 0:TS], wba_sb[:, dt * 8 + 4:dt * 8 + 8],
                                   hslab_b[:, dt * TS:(dt + 1) * TS],
                                   start=(dt == 0), stop=(dt == DT - 1))
                return i
            sch.task("tensor", t_proj_b, reads=["wba_sb", "hslab_b"], writes=["P4", "P5"],
                     tag="prjba" + sfx)

            def t_ev_ba(eng):
                eng.tensor_copy(out=braw[:], in_=P4[0:4, 0:TS])
                return eng.tensor_copy(out=araw[:], in_=P5[0:4, 0:TS])
            sch.task("vector", t_ev_ba, reads=["P4", "P5"], writes=["braw", "araw"],
                     tag="evba" + sfx)

            for nt in range(NT_MIX):
                def t_conv(eng, nt=nt):
                    base = nt * MW
                    w4 = convw_sb[:, nt * 4:(nt + 1) * 4]
                    eng.tensor_scalar_mul(out=convacc[:], in0=mixed[:, base:base + TS],
                                          scalar1=w4[:, 0:1])
                    for i in range(1, 4):
                        eng.scalar_tensor_tensor(out=convacc[:],
                                                 in0=mixed[:, base + i:base + i + TS],
                                                 scalar=w4[:, i:i + 1], in1=convacc[:],
                                                 op0=ALU.mult, op1=ALU.add)
                    return eng.tensor_copy(out=convo[:, nt * TS:(nt + 1) * TS], in_=convacc[:])
                sch.task("vector", t_conv, reads=[f"mixed{nt}", "convw_sb", "convacc"],
                         writes=["convacc", f"convo{nt}"], tag=f"cnv{nt}" + sfx)

            def t_silu(eng):
                eng.activation(out=convo[:], in_=convo[:], func=AF.Silu)
                return eng.activation(out=zsil[:], in_=zsil[:], func=AF.Silu)
            sch.task("scalar", t_silu,
                     reads=[f"convo{n}" for n in range(NT_MIX)] + [f"zsil{z}" for z in range(NT_Z)],
                     writes=[f"convo{n}" for n in range(NT_MIX)] + [f"zsil{z}" for z in range(NT_Z)],
                     tag="silu" + sfx)

            def t_sq_qk(eng):
                return eng.activation(out=sqbuf[:], in_=convo[:, 0:4 * TS], func=AF.Square)
            sch.task("scalar", t_sq_qk, reads=["convo0", "convo1", "convo2", "convo3"],
                     writes=["sqbuf"], tag="sqqk" + sfx)

            def t_mm_l2(eng):
                eng.matmul(P4[0:4, 0:TS], oh4_b[:, 0:4], sqbuf[:, 0:TS], start=True, stop=False)
                eng.matmul(P4[0:4, 0:TS], oh4_b[:, 4:8], sqbuf[:, TS:2 * TS], start=False, stop=True)
                eng.matmul(P5[0:4, 0:TS], oh4_b[:, 0:4], sqbuf[:, 2 * TS:3 * TS], start=True, stop=False)
                return eng.matmul(P5[0:4, 0:TS], oh4_b[:, 4:8], sqbuf[:, 3 * TS:4 * TS],
                                  start=False, stop=True)
            sch.task("tensor", t_mm_l2, reads=["oh4_b", "sqbuf"], writes=["P4", "P5"],
                     tag="mml2" + sfx)

            def t_uqk(eng):
                eng.tensor_copy(out=uq4[:], in_=P4[0:4, 0:TS])
                return eng.tensor_copy(out=uk4[:], in_=P5[0:4, 0:TS])
            sch.task("vector", t_uqk, reads=["P4", "P5"], writes=["uq4", "uk4"], tag="uqkc" + sfx)

            def t_uqk_ln(eng):
                eng.activation(out=uq4[:], in_=uq4[:], func=AF.Ln, bias=consts[0:4, 0:1])
                return eng.activation(out=uk4[:], in_=uk4[:], func=AF.Ln, bias=consts[0:4, 0:1])
            sch.task("scalar", t_uqk_ln, reads=["uq4", "uk4", "consts"],
                     writes=["uq4", "uk4"], tag="uqkln" + sfx)

            def t_ba1(eng):
                return eng.tensor_scalar_add(out=g4[:], in0=araw[:], scalar1=dtb_sb[:, 0:1])
            sch.task("vector", t_ba1, reads=["araw", "dtb_sb"], writes=["g4"], tag="ba1" + sfx)

            def t_ba2(eng):
                eng.activation(out=spbuf[:], in_=g4[:], func=AF.Abs)
                eng.activation(out=spbuf[:], in_=spbuf[:], func=AF.Exp, scale=-1.0)
                eng.activation(out=spbuf[:], in_=spbuf[:], func=AF.Ln, bias=1.0)
                return eng.activation(out=g4[:], in_=g4[:], func=AF.Relu)
            sch.task("scalar", t_ba2, reads=["g4"], writes=["g4", "spbuf"], tag="ba2" + sfx)

            def t_ba3(eng):
                eng.tensor_add(out=g4[:], in0=g4[:], in1=spbuf[:])
                return eng.tensor_scalar_mul(out=g4[:], in0=g4[:], scalar1=nega_sb[:, 0:1])
            sch.task("vector", t_ba3, reads=["g4", "spbuf", "nega"], writes=["g4"], tag="ba3" + sfx)

            def t_ba4(eng):
                eng.activation(out=beta4[:], in_=braw[:], func=AF.Sigmoid)
                eng.activation(out=spbuf[:], in_=braw[:], func=AF.Abs)
                eng.activation(out=spbuf[:], in_=spbuf[:], func=AF.Exp, scale=-1.0)
                eng.activation(out=spbuf[:], in_=spbuf[:], func=AF.Ln, bias=1.0)
                return eng.activation(out=lnb4[:], in_=braw[:], func=AF.Relu, scale=-1.0)
            sch.task("scalar", t_ba4, reads=["braw"], writes=["beta4", "spbuf", "lnb4"],
                     tag="ba4" + sfx)

            def t_ba5(eng):
                eng.tensor_add(out=lnb4[:], in0=lnb4[:], in1=spbuf[:])
                return eng.tensor_scalar_mul(out=lnb4[:], in0=lnb4[:], scalar1=-1.0)
            sch.task("vector", t_ba5, reads=["lnb4", "spbuf"], writes=["lnb4"], tag="ba5" + sfx)

            for cc in range(CPS):
                c0 = cc * C
                cfx = f"{sfx}c{cc}"

                def t_gT(eng, c0=c0):
                    return eng.transpose(P6[0:128, 0:4], g4[:, c0:c0 + C], ident_f[0:4, 0:4])
                sch.task("tensor", t_gT, reads=["g4", "ident_f"], writes=["P6"], tag="gT" + cfx)

                def t_gT_ev(eng):
                    return eng.tensor_copy(out=cvec[:, 0:4], in_=P6[0:128, 0:4])
                sch.task("vector", t_gT_ev, reads=["P6"], writes=["cvec"], tag="gTe" + cfx)

                def t_bfeat(eng):
                    return eng.matmul(P6[0:4, 128:256], cvec[:, 0:4], cums_f[:],
                                      start=True, stop=True)
                sch.task("tensor", t_bfeat, reads=["cvec", "cums_f"], writes=["P6"],
                         tag="bfeat" + cfx)

                def t_bfeat_ev(eng):
                    return eng.tensor_copy(out=bfeat[:], in_=P6[0:4, 128:256])
                sch.task("vector", t_bfeat_ev, reads=["P6"], writes=["bfeat"], tag="bfe" + cfx)

                def t_fvec(eng, c0=c0):
                    eng.scalar_tensor_tensor(out=cfA[:], in0=uk4[:, c0:c0 + C], scalar=-0.5,
                                             in1=bfeat[:], op0=ALU.mult, op1=ALU.subtract)
                    eng.tensor_add(out=rfA[:], in0=bfeat[:], in1=lnb4[:, c0:c0 + C])
                    eng.scalar_tensor_tensor(out=rfA[:], in0=uk4[:, c0:c0 + C], scalar=-0.5,
                                             in1=rfA[:], op0=ALU.mult, op1=ALU.add)
                    eng.scalar_tensor_tensor(out=rfP[:], in0=uq4[:, c0:c0 + C], scalar=-0.5,
                                             in1=bfeat[:], op0=ALU.mult, op1=ALU.add)
                    eng.tensor_scalar_add(out=rfP[:], in0=rfP[:], scalar1=LOG_SCALE_Q)
                    return eng.tensor_scalar_add(out=gfacf[:], in0=cfA[:],
                                                 scalar1=bfeat[:, 127:128])
                sch.task("vector", t_fvec, reads=["uk4", "uq4", "lnb4", "bfeat"],
                         writes=["cfA", "rfA", "rfP", "gfacf"], tag="fvec" + cfx)

                def t_facts(eng):
                    eng.activation(out=rfacf[:], in_=rfA[:], func=AF.Exp)
                    eng.activation(out=pfacf[:], in_=rfP[:], func=AF.Exp)
                    eng.activation(out=gfacf[:], in_=gfacf[:], func=AF.Exp)
                    return eng.activation(out=ebcf[:], in_=bfeat[:, 127:128], func=AF.Exp)
                sch.task("scalar", t_facts, reads=["rfA", "rfP", "gfacf", "bfeat"],
                         writes=["rfacf", "pfacf", "gfacf", "ebcf"], tag="facts" + cfx)

                def t_vecT(eng, c0=c0):
                    id4 = ident_f[0:4, 0:4]
                    eng.transpose(P6[0:128, 4:8], cfA[:], id4)
                    eng.transpose(P6[0:128, 8:12], rfA[:], id4)
                    eng.transpose(P6[0:128, 12:16], rfacf[:], id4)
                    eng.transpose(P6[0:128, 16:20], pfacf[:], id4)
                    eng.transpose(P6[0:128, 20:24], gfacf[:], id4)
                    eng.transpose(P6[0:128, 24:28], beta4[:, c0:c0 + C], id4)
                    for h in range(4):
                        i = eng.matmul(P6[:, 28 + h:29 + h], sel4_f[:, h * 128:(h + 1) * 128],
                                       ebcf[:], start=True, stop=True)
                    return i
                sch.task("tensor", t_vecT, reads=["cfA", "rfA", "rfacf", "pfacf", "gfacf",
                                                  "beta4", "ebcf", "ident_f", "sel4_f"],
                         writes=["P6"], tag="vecT" + cfx)

                def t_vecT_ev(eng):
                    return eng.tensor_copy(out=cvec[:, 4:32], in_=P6[0:128, 4:32])
                sch.task("vector", t_vecT_ev, reads=["P6"], writes=["cvec"], tag="vecTe" + cfx)

                def t_grams(eng, c0=c0):
                    for kq in range(2):
                        kt = convo[:, (2 + kq) * TS + c0:(2 + kq) * TS + c0 + C]
                        qt = convo[:, kq * TS + c0:kq * TS + c0 + C]
                        eng.matmul(P0[:, kq * 128:(kq + 1) * 128], kt, kt, start=True, stop=True)
                        i = eng.matmul(P1[:, kq * 128:(kq + 1) * 128], kt, qt, start=True, stop=True)
                    return i
                sch.task("tensor", t_grams, reads=["convo0", "convo1", "convo2", "convo3"],
                         writes=["P0", "P1"], tag="grams" + cfx)

                def t_gram_ev(eng):
                    eng.tensor_copy(out=kgb[:], in_=P0[:, 0:256])
                    return eng.tensor_copy(out=qkgb[:], in_=P1[:, 0:256])
                sch.task("vector", t_gram_ev, reads=["P0", "P1"], writes=["kgb", "qkgb"],
                         tag="gramev" + cfx)

                def t_bc3(eng):
                    for h in range(4):
                        sl = sel4_f[:, h * 128:(h + 1) * 128]
                        eng.matmul(P0[:, h * 128:(h + 1) * 128], sl, cfA[:], start=True, stop=True)
                        eng.matmul(P1[:, h * 128:(h + 1) * 128], sl, rfA[:], start=True, stop=True)
                        i = eng.matmul(P2[:, h * 128:(h + 1) * 128], sl, rfP[:], start=True, stop=True)
                    return i
                sch.task("tensor", t_bc3, reads=["sel4_f", "cfA", "rfA", "rfP"],
                         writes=["P0", "P1", "P2"], tag="bc3" + cfx)

                def t_bldA(eng):
                    for h in range(4):
                        eng.tensor_scalar_add(out=argbuf[:, h * 128:(h + 1) * 128],
                                              in0=P0[:, h * 128:(h + 1) * 128],
                                              scalar1=cvec[:, 8 + h:9 + h])
                    return eng.tensor_add(out=argbuf[:], in0=argbuf[:], in1=masks4_f[:, 0:512])
                sch.task("vector", t_bldA, reads=["P0", "cvec", "masks4_f"], writes=["argbuf"],
                         tag="bldA" + cfx)

                def t_expA(eng):
                    return eng.activation(out=argbuf[:], in_=argbuf[:], func=AF.Exp)
                sch.task("scalar", t_expA, reads=["argbuf"], writes=["argbuf"], tag="expA" + cfx)

                def t_mulA(eng):
                    for h in range(4):
                        kq = h // 2
                        i = eng.tensor_mul(out=Abf[:, h * 128:(h + 1) * 128],
                                           in0=argbuf[:, h * 128:(h + 1) * 128],
                                           in1=kgb[:, kq * 128:(kq + 1) * 128])
                    return i
                sch.task("vector", t_mulA, reads=["argbuf", "kgb"], writes=["Abf"], tag="mulA" + cfx)

                def t_bldAT(eng):
                    for h in range(4):
                        eng.tensor_scalar_add(out=argbuf[:, h * 128:(h + 1) * 128],
                                              in0=P1[:, h * 128:(h + 1) * 128],
                                              scalar1=cvec[:, 4 + h:5 + h])
                    return eng.tensor_add(out=argbuf[:], in0=argbuf[:], in1=masks4_f[:, 512:1024])
                sch.task("vector", t_bldAT, reads=["P1", "cvec", "masks4_f"], writes=["argbuf"],
                         tag="bldAT" + cfx)

                def t_expAT(eng):
                    return eng.activation(out=argbuf[:], in_=argbuf[:], func=AF.Exp)
                sch.task("scalar", t_expAT, reads=["argbuf"], writes=["argbuf"], tag="expAT" + cfx)

                def t_mulAT(eng):
                    for h in range(4):
                        kq = h // 2
                        i = eng.tensor_mul(out=ATbf[:, h * 128:(h + 1) * 128],
                                           in0=argbuf[:, h * 128:(h + 1) * 128],
                                           in1=kgb[:, kq * 128:(kq + 1) * 128])
                    return i
                sch.task("vector", t_mulAT, reads=["argbuf", "kgb"], writes=["ATbf"],
                         tag="mulAT" + cfx)

                def t_bldPT(eng):
                    for h in range(4):
                        eng.tensor_scalar_add(out=argbuf[:, h * 128:(h + 1) * 128],
                                              in0=P2[:, h * 128:(h + 1) * 128],
                                              scalar1=cvec[:, 4 + h:5 + h])
                    return eng.tensor_add(out=argbuf[:], in0=argbuf[:], in1=masks4_f[:, 1024:1536])
                sch.task("vector", t_bldPT, reads=["P2", "cvec", "masks4_f"], writes=["argbuf"],
                         tag="bldPT" + cfx)

                def t_expPT(eng):
                    return eng.activation(out=argbuf[:], in_=argbuf[:], func=AF.Exp)
                sch.task("scalar", t_expPT, reads=["argbuf"], writes=["argbuf"], tag="expPT" + cfx)

                def t_mulPT(eng):
                    for h in range(4):
                        kq = h // 2
                        i = eng.tensor_mul(out=Pm[:, h * 128:(h + 1) * 128],
                                           in0=argbuf[:, h * 128:(h + 1) * 128],
                                           in1=qkgb[:, kq * 128:(kq + 1) * 128])
                    return i
                sch.task("vector", t_mulPT, reads=["argbuf", "qkgb"], writes=["Pm"],
                         tag="mulPT" + cfx)

                def t_inv_init(eng):
                    for h in range(4):
                        i = eng.scalar_tensor_tensor(out=PTm[:, h * 128:(h + 1) * 128],
                                                     in0=ATbf[:, h * 128:(h + 1) * 128],
                                                     scalar=-1.0, in1=ident_b[:],
                                                     op0=ALU.mult, op1=ALU.add)
                    return i
                sch.task("vector", t_inv_init, reads=["ATbf", "ident_b"], writes=["PTm"],
                         tag="invini" + cfx)

                cur, curT = "Abf", "ATbf"
                cur_sb, curT_sb = Abf, ATbf
                nxt = [(SqA, SqAT, "SqA", "SqAT"), (SqB, SqBT, "SqB", "SqBT")]
                for lev in range(6):
                    nsb, nsbT, nn, nnT = nxt[lev % 2]
                    last = lev == 5
                    def t_sqr(eng, cur_sb=cur_sb, curT_sb=curT_sb, last=last):
                        for h in range(4):
                            i = eng.matmul(P0[:, h * 128:(h + 1) * 128],
                                           curT_sb[:, h * 128:(h + 1) * 128],
                                           cur_sb[:, h * 128:(h + 1) * 128], start=True, stop=True)
                            if not last:
                                i = eng.matmul(P1[:, h * 128:(h + 1) * 128],
                                               cur_sb[:, h * 128:(h + 1) * 128],
                                               curT_sb[:, h * 128:(h + 1) * 128],
                                               start=True, stop=True)
                        return i
                    sch.task("tensor", t_sqr, reads=[cur, curT],
                             writes=["P0", "P1"] if not last else ["P0"],
                             tag=f"sq{lev}" + cfx)

                    def t_sqr_ev(eng, nsb=nsb, nsbT=nsbT, last=last):
                        i = eng.tensor_copy(out=nsb[:], in_=P0[:])
                        if not last:
                            i = eng.tensor_copy(out=nsbT[:], in_=P1[:])
                        return i
                    sch.task("vector", t_sqr_ev, reads=["P0", "P1"] if not last else ["P0"],
                             writes=[nn, nnT] if not last else [nn],
                             tag=f"sqe{lev}" + cfx)

                    def t_ptup(eng, nsb=nsb):
                        for h in range(4):
                            i = eng.matmul(P3[:, h * 128:(h + 1) * 128],
                                           nsb[:, h * 128:(h + 1) * 128],
                                           PTm[:, h * 128:(h + 1) * 128], start=True, stop=True)
                        return i
                    sch.task("tensor", t_ptup, reads=[nn, "PTm"], writes=["P3"],
                             tag=f"pt{lev}" + cfx)

                    def t_ptadd(eng):
                        return eng.tensor_add(out=PTm[:], in0=PTm[:], in1=P3[:])
                    sch.task("vector", t_ptadd, reads=["P3", "PTm"], writes=["PTm"],
                             tag=f"pta{lev}" + cfx)

                    cur, curT, cur_sb, curT_sb = nn, nnT, nsb, nsbT

                def t_vT(eng, c0=c0):
                    for h in range(4):
                        i = eng.transpose(PB7[:, h * 128:(h + 1) * 128],
                                          convo[:, (4 + h) * TS + c0:(4 + h) * TS + c0 + C],
                                          ident_b[:])
                    return i
                sch.task("tensor", t_vT, reads=["convo4", "convo5", "convo6", "convo7", "ident_b"],
                         writes=["PB7"], tag="vT" + cfx)

                def t_vT_ev(eng):
                    return eng.tensor_copy(out=Vtok[:], in_=PB7[:, 0:512])
                sch.task("vector", t_vT_ev, reads=["PB7"], writes=["Vtok"], tag="vTe" + cfx)

                def t_R(eng, c0=c0):
                    for h in range(4):
                        kq = h // 2
                        kt = convo[:, (2 + kq) * TS + c0:(2 + kq) * TS + c0 + C]
                        i = eng.matmul(P0[:, h * 128:(h + 1) * 128], kt,
                                       S_bf[:, h * 128:(h + 1) * 128], start=True, stop=True)
                    return i
                sch.task("tensor", t_R, reads=["convo2", "convo3", "S_bf"], writes=["P0"],
                         tag="R" + cfx)

                def t_Vr(eng):
                    for h in range(4):
                        eng.tensor_scalar_mul(out=gtmp[:, h * 128:(h + 1) * 128],
                                              in0=P0[:, h * 128:(h + 1) * 128],
                                              scalar1=cvec[:, 12 + h:13 + h])
                        i = eng.scalar_tensor_tensor(out=Vr[:, h * 128:(h + 1) * 128],
                                                     in0=Vtok[:, h * 128:(h + 1) * 128],
                                                     scalar=cvec[:, 24 + h:25 + h],
                                                     in1=gtmp[:, h * 128:(h + 1) * 128],
                                                     op0=ALU.mult, op1=ALU.subtract)
                    return i
                sch.task("vector", t_Vr, reads=["P0", "cvec", "Vtok"], writes=["Vr", "gtmp"],
                         tag="Vr" + cfx)

                def t_U(eng):
                    for h in range(4):
                        i = eng.matmul(P1[:, h * 128:(h + 1) * 128],
                                       PTm[:, h * 128:(h + 1) * 128],
                                       Vr[:, h * 128:(h + 1) * 128], start=True, stop=True)
                    return i
                sch.task("tensor", t_U, reads=["PTm", "Vr"], writes=["P1"], tag="U" + cfx)

                def t_U_ev(eng):
                    return eng.tensor_copy(out=Ubf[:], in_=P1[:])
                sch.task("vector", t_U_ev, reads=["P1"], writes=["Ubf"], tag="Ue" + cfx)

                def t_oi(eng, c0=c0):
                    for h in range(4):
                        kq = h // 2
                        qt = convo[:, kq * TS + c0:kq * TS + c0 + C]
                        eng.matmul(P2[:, h * 128:(h + 1) * 128], qt,
                                   S_bf[:, h * 128:(h + 1) * 128], start=True, stop=True)
                        i = eng.matmul(P3[:, h * 128:(h + 1) * 128],
                                       Pm[:, h * 128:(h + 1) * 128],
                                       Ubf[:, h * 128:(h + 1) * 128], start=True, stop=True)
                    return i
                sch.task("tensor", t_oi, reads=["convo0", "convo1", "S_bf", "Pm", "Ubf"],
                         writes=["P2", "P3"], tag="oi" + cfx)

                def t_oev(eng):
                    for h in range(4):
                        eng.tensor_scalar_mul(out=otok[:, h * 128:(h + 1) * 128],
                                              in0=P2[:, h * 128:(h + 1) * 128],
                                              scalar1=cvec[:, 16 + h:17 + h])
                    return eng.tensor_add(out=otok[:], in0=otok[:], in1=P3[:])
                sch.task("vector", t_oev, reads=["P2", "P3", "cvec"], writes=["otok"],
                         tag="oev" + cfx)

                def t_kT(eng, c0=c0):
                    for kq in range(2):
                        kt = convo[:, (2 + kq) * TS + c0:(2 + kq) * TS + c0 + C]
                        i = eng.transpose(PB7[:, 512 + kq * 128:512 + (kq + 1) * 128], kt, ident_b[:])
                    return i
                sch.task("tensor", t_kT, reads=["convo2", "convo3", "ident_b"], writes=["PB7"],
                         tag="kT" + cfx)

                def t_kT_ev(eng):
                    return eng.tensor_copy(out=ktokb[:], in_=PB7[:, 512:768])
                sch.task("vector", t_kT_ev, reads=["PB7"], writes=["ktokb"], tag="kTe" + cfx)

                def t_kbar(eng):
                    for h in range(4):
                        kq = h // 2
                        i = eng.tensor_scalar_mul(out=kbar[:, h * 128:(h + 1) * 128],
                                                  in0=ktokb[:, kq * 128:(kq + 1) * 128],
                                                  scalar1=cvec[:, 20 + h:21 + h])
                    return i
                sch.task("vector", t_kbar, reads=["ktokb", "cvec"], writes=["kbar"],
                         tag="kbar" + cfx)

                def t_dS(eng):
                    for h in range(4):
                        i = eng.matmul(P0[:, h * 128:(h + 1) * 128],
                                       kbar[:, h * 128:(h + 1) * 128],
                                       Ubf[:, h * 128:(h + 1) * 128], start=True, stop=True)
                    return i
                sch.task("tensor", t_dS, reads=["kbar", "Ubf"], writes=["P0"], tag="dS" + cfx)

                def t_Supd(eng):
                    for h in range(4):
                        eng.scalar_tensor_tensor(out=S_sb[:, h * 128:(h + 1) * 128],
                                                 in0=S_sb[:, h * 128:(h + 1) * 128],
                                                 scalar=cvec[:, 28 + h:29 + h],
                                                 in1=P0[:, h * 128:(h + 1) * 128],
                                                 op0=ALU.mult, op1=ALU.add)
                    return eng.tensor_copy(out=S_bf[:], in_=S_sb[:])
                sch.task("vector", t_Supd, reads=["P0", "S", "cvec"], writes=["S", "S_bf"],
                         tag="Supd" + cfx)

                def t_zT(eng, c0=c0):
                    for h in range(4):
                        i = eng.transpose(PB7[:, h * 128:(h + 1) * 128],
                                          zsil[:, h * TS + c0:h * TS + c0 + C], ident_b[:])
                    return i
                sch.task("tensor", t_zT, reads=["zsil0", "zsil1", "zsil2", "zsil3", "ident_b"],
                         writes=["PB7"], tag="zT" + cfx)

                def t_zT_ev(eng):
                    return eng.tensor_copy(out=ztok[:], in_=PB7[:, 0:512])
                sch.task("vector", t_zT_ev, reads=["PB7"], writes=["ztok"], tag="zTe" + cfx)

                def t_gat_sq(eng):
                    return eng.activation(out=gtmp[:], in_=otok[:], func=AF.Square)
                sch.task("scalar", t_gat_sq, reads=["otok"], writes=["gtmp"], tag="gsq" + cfx)

                def t_gat_sum(eng):
                    for h in range(4):
                        i = eng.reduce_sum(out=ssum[:, h:h + 1], in_=gtmp[:, h * 128:(h + 1) * 128],
                                           axis=AX.X)
                    return i
                sch.task("vector", t_gat_sum, reads=["gtmp"], writes=["ssum"], tag="gsum" + cfx)

                def t_gat_rstd(eng):
                    eng.activation(out=rstd[:], in_=ssum[:], func=AF.Ln,
                                   scale=1.0 / DV, bias=consts[:, 0:1])
                    return eng.activation(out=rstd[:], in_=rstd[:], func=AF.Exp, scale=-0.5)
                sch.task("scalar", t_gat_rstd, reads=["ssum", "consts"], writes=["rstd"],
                         tag="grstd" + cfx)

                def t_gate(eng):
                    for h in range(4):
                        eng.tensor_scalar_mul(out=gatedtok[:, h * 128:(h + 1) * 128],
                                              in0=otok[:, h * 128:(h + 1) * 128],
                                              scalar1=rstd[:, h:h + 1])
                    eng.tensor_mul(out=gatedtok[:], in0=gatedtok[:], in1=gnwb[:])
                    return eng.tensor_mul(out=gatedtok[:], in0=gatedtok[:], in1=ztok[:])
                sch.task("vector", t_gate, reads=["otok", "rstd", "gnwb", "ztok"],
                         writes=["gatedtok"], tag="gate" + cfx)

                def t_gTr(eng):
                    for h in range(4):
                        i = eng.transpose(PB7[:, h * 128:(h + 1) * 128],
                                          gatedtok[:, h * 128:(h + 1) * 128], ident_b[:])
                    return i
                sch.task("tensor", t_gTr, reads=["gatedtok", "ident_b"], writes=["PB7"],
                         tag="gTr" + cfx)

                def t_gTr_ev(eng, c0=c0):
                    for h in range(4):
                        i = eng.tensor_copy(out=gatedf[:, h * TS + c0:h * TS + c0 + C],
                                            in_=PB7[:, h * 128:(h + 1) * 128])
                    return i
                sch.task("vector", t_gTr_ev, reads=["PB7"], writes=["gatedf"], tag="gTre" + cfx)

            def t_store(eng, dsem, t0=t0):
                eng.dma_start(out=gatedT_t[:, t0:t0 + TS].rearrange("(t p) n -> p t n", p=128),
                              in_=gatedf[:].rearrange("p (t n) -> p t n", t=4)).then_inc(dsem, 16)
                return 1
            sch.task("sync", t_store, reads=["gatedf"], dma=1, tag="st" + sfx)

        @block.sync
        def _(eng):
            sch.run("sync", eng)

        @block.gpsimd
        def _(eng):
            sch.run("gpsimd", eng)

        @block.tensor
        def _(eng):
            sch.run("tensor", eng)

        @block.vector
        def _(eng):
            sch.run("vector", eng)

        @block.scalar
        def _(eng):
            sch.run("scalar", eng)
    return nc


def run_launch_A(inp):
    per_core = stage_launch_A(inp)
    nc = build_launch_A()
    res = run_bass_kernel_spmd(nc, per_core, core_ids=list(range(NCORE)))
    return np.concatenate([res.results[c]["gatedT"] for c in range(NCORE)], axis=0)




def stage_launch_B(inp, gatedT):
    xT = _f32(np.asarray(inp["L0_attn_hidden_states"])[0].T)
    Wout = _f32(inp["L0_attn_W_out"])
    ln2 = _f32(inp["L0_attn_ln2_w"])
    Wg = _f32(inp["L0_attn_W_gate"])
    Wu = _f32(inp["L0_attn_W_up"])
    Wd = _f32(inp["L0_attn_W_down"])
    w8g = np.ascontiguousarray(np.asarray(inp["L0_ffn_w_gate"]))
    w8u = np.ascontiguousarray(np.asarray(inp["L0_ffn_w_up"]))
    w8d = np.ascontiguousarray(np.asarray(inp["L0_ffn_w_down"]))
    sg = _f32(np.asarray(inp["L0_ffn_s_gate"]).reshape(64, 128).T)
    su = _f32(np.asarray(inp["L0_ffn_s_up"]).reshape(64, 128).T)
    sd = _f32(np.asarray(inp["L0_ffn_s_down"]).reshape(16, 128).T)
    per_core = []
    T = 1024
    for c in range(NCORE):
        sl = slice(c * T, (c + 1) * T)
        per_core.append({
            "xs": _f32(xT[:, sl]), "gs": _f32(gatedT[:, sl]),
            "wout": Wout, "ln2w": _f32(ln2.reshape(DT, 128).T),
            "wg": Wg, "wu": Wu, "wd": Wd,
            "w8g": w8g, "w8u": w8u, "w8d": w8d,
            "sg": sg, "su": su, "sd": sd,
        })
    return per_core


class Ctx:

    def __init__(self, nc, sch, st):
        self.nc = nc
        self.sch = sch
        self.consts = st.enter_context(nc.sbuf_tensor("cst", [128, 4], F32))
        self.ones1f = st.enter_context(nc.sbuf_tensor("o1f", [1, 128], F32))
        self.onescol_b = st.enter_context(nc.sbuf_tensor("ocb", [128, 1], BF16))
        self.stats = st.enter_context(nc.sbuf_tensor("stats", [1, 4 * 512], F32))

    def init_tasks(self):
        c = self.consts

        def t_ini(eng):
            eng.memset(self.ones1f[:], 1.0)
            eng.memset(self.onescol_b[:], 1.0)
            eng.memset(c[:, 0:1], EPS)
            eng.memset(c[:, 1:2], LN_EPS)
            return eng.memset(c[:, 2:3], 0.0)
        self.sch.task("vector", t_ini, writes=["cst", "o1f", "ocb"], tag="cini")

    def stats_pipeline(self, tile_ap, src_name, ntiles, W, sqb, sq_name, ps_sum, ps_sum_n,
                       ps_rs, ps_rs_n, tag, use_mean=True, eps_col=1):
        sch, stats, consts = self.sch, self.stats, self.consts
        T1 = W

        def t_sq(eng):
            for dt in range(ntiles):
                i = eng.activation(out=sqb[:, dt * W:(dt + 1) * W], in_=tile_ap(dt),
                                   func=AF.Square)
            return i
        sch.task("scalar", t_sq, reads=[src_name], writes=[sq_name], tag="sq" + tag)

        def t_sums(eng):
            if use_mean:
                for dt in range(ntiles):
                    eng.matmul(ps_sum[0:1, 0:W], self.onescol_b[:], tile_ap(dt),
                               start=(dt == 0), stop=(dt == ntiles - 1))
            for dt in range(ntiles):
                i = eng.matmul(ps_rs[0:1, 0:W], self.onescol_b[:], sqb[:, dt * W:(dt + 1) * W],
                               start=(dt == 0), stop=(dt == ntiles - 1))
            return i
        sch.task("tensor", t_sums, reads=["ocb", src_name, sq_name],
                 writes=[ps_sum_n, ps_rs_n] if use_mean else [ps_rs_n], tag="sums" + tag)

        nd = ntiles * 128
        if use_mean:
            def t_s1(eng):
                eng.tensor_scalar_mul(out=stats[0:1, 0:T1], in0=ps_sum[0:1, 0:W], scalar1=1.0 / nd)
                eng.tensor_scalar_mul(out=stats[0:1, T1:2 * T1], in0=ps_rs[0:1, 0:W], scalar1=1.0 / nd)
                eng.tensor_mul(out=stats[0:1, 2 * T1:3 * T1], in0=stats[0:1, 0:T1], in1=stats[0:1, 0:T1])
                return eng.tensor_sub(out=stats[0:1, T1:2 * T1], in0=stats[0:1, T1:2 * T1],
                                      in1=stats[0:1, 2 * T1:3 * T1])
            sch.task("vector", t_s1, reads=[ps_sum_n, ps_rs_n], writes=["stats"], tag="s1" + tag)

            def t_s2(eng):
                eng.activation(out=stats[0:1, 2 * T1:3 * T1], in_=stats[0:1, T1:2 * T1],
                               func=AF.Ln, bias=consts[0:1, eps_col:eps_col + 1])
                return eng.activation(out=stats[0:1, 2 * T1:3 * T1], in_=stats[0:1, 2 * T1:3 * T1],
                                      func=AF.Exp, scale=-0.5)
            sch.task("scalar", t_s2, reads=["stats", "cst"], writes=["stats"], tag="s2" + tag)

            def t_s3(eng):
                eng.tensor_mul(out=stats[0:1, 3 * T1:4 * T1], in0=stats[0:1, 0:T1],
                               in1=stats[0:1, 2 * T1:3 * T1])
                return eng.tensor_scalar_mul(out=stats[0:1, 3 * T1:4 * T1],
                                             in0=stats[0:1, 3 * T1:4 * T1], scalar1=-1.0)
            sch.task("vector", t_s3, reads=["stats"], writes=["stats"], tag="s3" + tag)
        else:
            def t_s1(eng):
                return eng.tensor_scalar_mul(out=stats[0:1, T1:2 * T1], in0=ps_rs[0:1, 0:W],
                                             scalar1=1.0 / nd)
            sch.task("vector", t_s1, reads=[ps_rs_n], writes=["stats"], tag="s1" + tag)

            def t_s2(eng):
                eng.activation(out=stats[0:1, 2 * T1:3 * T1], in_=stats[0:1, T1:2 * T1],
                               func=AF.Ln, bias=consts[0:1, eps_col:eps_col + 1])
                return eng.activation(out=stats[0:1, 2 * T1:3 * T1], in_=stats[0:1, 2 * T1:3 * T1],
                                      func=AF.Exp, scale=-0.5)
            sch.task("scalar", t_s2, reads=["stats", "cst"], writes=["stats"], tag="s2" + tag)

        def t_bc(eng):
            i = eng.matmul(ps_rs[:, 0:W], self.ones1f[0:1, :], stats[0:1, 2 * T1:3 * T1],
                           start=True, stop=True)
            if use_mean:
                i = eng.matmul(ps_sum[:, 0:W], self.ones1f[0:1, :], stats[0:1, 3 * T1:4 * T1],
                               start=True, stop=True)
            return i
        sch.task("tensor", t_bc, reads=["o1f", "stats"],
                 writes=[ps_rs_n, ps_sum_n] if use_mean else [ps_rs_n], tag="bc" + tag)


def build_launch_B(debug=False):
    nc = bass.Bass("TRN2", target_bir_lowering=False)
    T = 1024
    TH = 512
    xs_t = nc.dram_tensor("xs", [D, T], F32, kind="ExternalInput").ap()
    gs_t = nc.dram_tensor("gs", [VAL_DIM, T], F32, kind="ExternalInput").ap()
    wout_t = nc.dram_tensor("wout", [VAL_DIM, D], F32, kind="ExternalInput").ap()
    ln2w_t = nc.dram_tensor("ln2w", [128, DT], F32, kind="ExternalInput").ap()
    wg_t = nc.dram_tensor("wg", [D, 5120], F32, kind="ExternalInput").ap()
    wu_t = nc.dram_tensor("wu", [D, 5120], F32, kind="ExternalInput").ap()
    wd_t = nc.dram_tensor("wd", [5120, D], F32, kind="ExternalInput").ap()
    w8g_t = nc.dram_tensor("w8g", [D, 8192], I8, kind="ExternalInput").ap()
    w8u_t = nc.dram_tensor("w8u", [D, 8192], I8, kind="ExternalInput").ap()
    w8d_t = nc.dram_tensor("w8d", [8192, D], I8, kind="ExternalInput").ap()
    sg_t = nc.dram_tensor("sg", [128, 64], F32, kind="ExternalInput").ap()
    su_t = nc.dram_tensor("su", [128, 64], F32, kind="ExternalInput").ap()
    sd_t = nc.dram_tensor("sd", [128, 16], F32, kind="ExternalInput").ap()
    x2_t = nc.dram_tensor("x2", [D, T], F32, kind="ExternalOutput").ap()
    if debug:
        dbg1_t = nc.dram_tensor("dbg1", [D, T], F32, kind="ExternalOutput").ap()
        dbg2_t = nc.dram_tensor("dbg2", [D, T], F32, kind="ExternalOutput").ap()
        dbg3_t = nc.dram_tensor("dbg3", [D, T], F32, kind="ExternalOutput").ap()
        dbg4_t = nc.dram_tensor("dbg4", [D, T], F32, kind="ExternalOutput").ap()

    NIT_MLP = 40
    NIT_FFN = 64

    from contextlib import ExitStack
    with ExitStack() as st:
        xb = st.enter_context(nc.sbuf_tensor("xb", [128, DT * T], BF16))
        gatedb = st.enter_context(nc.sbuf_tensor("gatedb", [128, 32 * TH], BF16))
        hmid = st.enter_context(nc.sbuf_tensor("hmid", [128, DT * TH], BF16))
        h2b = st.enter_context(nc.sbuf_tensor("h2b", [128, DT * TH], BF16))
        big = st.enter_context(nc.sbuf_tensor("big", [128, NIT_FFN * TH], BF16))
        sqb = big
        tmp1 = st.enter_context(nc.sbuf_tensor("tmp1", [128, TH], BF16))
        tmp2 = st.enter_context(nc.sbuf_tensor("tmp2", [128, TH], BF16))
        stg0 = st.enter_context(nc.sbuf_tensor("stg0", [128, TH], F32))
        stg1 = st.enter_context(nc.sbuf_tensor("stg1", [128, TH], F32))
        ln2_sb = st.enter_context(nc.sbuf_tensor("ln2_sb", [128, DT], F32))
        sgsb = st.enter_context(nc.sbuf_tensor("sgsb", [128, 64], F32))
        susb = st.enter_context(nc.sbuf_tensor("susb", [128, 64], F32))
        sdsb = st.enter_context(nc.sbuf_tensor("sdsb", [128, 16], F32))
        wb0 = st.enter_context(nc.sbuf_tensor("wb0", [128, 64 * 128], BF16))
        wb1 = st.enter_context(nc.sbuf_tensor("wb1", [128, 64 * 128], BF16))
        P0 = st.enter_context(nc.psum_tensor("P0", [128, 512], F32))
        P1 = st.enter_context(nc.psum_tensor("P1", [128, 512], F32))
        P2 = st.enter_context(nc.psum_tensor("P2", [128, 512], F32))
        P3 = st.enter_context(nc.psum_tensor("P3", [128, 512], F32))
        P4 = st.enter_context(nc.psum_tensor("P4", [128, 512], F32))
        P5 = st.enter_context(nc.psum_tensor("P5", [128, 512], F32))
        block = st.enter_context(nc.Block())
        sch = Sched(nc)
        ctx = Ctx(nc, sch, st)
        ctx.init_tasks()
        PS = [P0, P1, P2, P3]
        psn = ["P0", "P1", "P2", "P3"]
        WB = [(wb0, "wb0"), (wb1, "wb1")]

        def t_pre(eng, dsem):
            eng.dma_start(out=xb[:].rearrange("p (t n) -> p t n", t=DT),
                          in_=xs_t.rearrange("(t p) n -> p t n", p=128)).then_inc(dsem, 16)
            eng.dma_start(out=ln2_sb[:], in_=ln2w_t).then_inc(dsem, 16)
            eng.dma_start(out=sgsb[:], in_=sg_t).then_inc(dsem, 16)
            eng.dma_start(out=susb[:], in_=su_t).then_inc(dsem, 16)
            eng.dma_start(out=sdsb[:], in_=sd_t).then_inc(dsem, 16)
            return 5
        sch.task("gpsimd", t_pre, writes=["xb", "ln2", "sg", "su", "sd"], dma=5, tag="preB")

        ndma_out = 0
        for th in range(2):
            fx = f"h{th}"
            xt = lambda dt, th=th: xb[:, dt * T + th * TH: dt * T + (th + 1) * TH]

            ctx.stats_pipeline(xt, "xb", DT, TH, sqb, "big", P4, "P4", P5, "P5",
                               "lnx" + fx, use_mean=True)

            def t_ldg(eng, dsem, th=th):
                eng.dma_start(out=gatedb[:].rearrange("p (t n) -> p t n", t=32),
                              in_=gs_t[:, th * TH:(th + 1) * TH]
                              .rearrange("(t p) n -> p t n", p=128)).then_inc(dsem, 16)
                return 1
            sch.task("gpsimd", t_ldg, writes=["gatedb"], dma=1, tag="ldg" + fx)

            for nt in range(16):
                wsb, wn = WB[nt % 2]
                def t_ldw(eng, dsem, nt=nt, wsb=wsb):
                    eng.dma_start(out=wsb[:, 0:32 * 128].rearrange("p (v n) -> p v n", v=32),
                                  in_=wout_t[:, nt * 128:(nt + 1) * 128]
                                  .rearrange("(v p) n -> p v n", p=128)).then_inc(dsem, 16)
                    return 1
                sch.task("gpsimd", t_ldw, writes=[wn], dma=1, tag=f"ldwo{nt}" + fx)

                pb, pn = PS[nt % 4], psn[nt % 4]
                def t_mm(eng, wsb=wsb, pb=pb):
                    for vt in range(32):
                        i = eng.matmul(pb[:, 0:TH], wsb[:, vt * 128:(vt + 1) * 128],
                                       gatedb[:, vt * TH:(vt + 1) * TH],
                                       start=(vt == 0), stop=(vt == 31))
                    return i
                sch.task("tensor", t_mm, reads=[wn, "gatedb"], writes=[pn], tag=f"mmwo{nt}" + fx)

                def t_ev(eng, nt=nt, pb=pb, xt=xt):
                    eng.tensor_mul(out=tmp1[:], in0=xt(nt), in1=P5[:, 0:TH])
                    eng.tensor_add(out=tmp1[:], in0=tmp1[:], in1=P4[:, 0:TH])
                    return eng.tensor_add(out=hmid[:, nt * TH:(nt + 1) * TH],
                                          in0=tmp1[:], in1=pb[:, 0:TH])
                sch.task("vector", t_ev, reads=[pn, "P4", "P5", "xb"],
                         writes=["tmp1", "hmid"], tag=f"evwo{nt}" + fx)

            if debug:
                def t_dbg1(eng, dsem, th=th):
                    eng.dma_start(out=dbg1_t[:, th * TH:(th + 1) * TH]
                                  .rearrange("(t p) n -> p t n", p=128),
                                  in_=hmid[:].rearrange("p (t n) -> p t n", t=DT)).then_inc(dsem, 16)
                    return 1
                sch.task("gpsimd", t_dbg1, reads=["hmid"], dma=1, tag="dbg1" + fx)

            ht = lambda dt: hmid[:, dt * TH:(dt + 1) * TH]
            ctx.stats_pipeline(ht, "hmid", DT, TH, sqb, "big", P4, "P4", P5, "P5",
                               "rms2" + fx, use_mean=False, eps_col=0)

            def t_h2(eng):
                for dt in range(DT):
                    eng.tensor_mul(out=tmp1[:], in0=hmid[:, dt * TH:(dt + 1) * TH], in1=P5[:, 0:TH])
                    i = eng.tensor_scalar_mul(out=h2b[:, dt * TH:(dt + 1) * TH],
                                              in0=tmp1[:], scalar1=ln2_sb[:, dt:dt + 1])
                return i
            sch.task("vector", t_h2, reads=["hmid", "P5", "ln2"], writes=["h2b", "tmp1"],
                     tag="h2" + fx)

            for it in range(NIT_MLP):
                wsb, wn = WB[it % 2]
                def t_ldw(eng, dsem, it=it, wsb=wsb):
                    eng.dma_start(out=wsb[:, 0:16 * 128].rearrange("p (d n) -> p d n", d=16),
                                  in_=wg_t[:, it * 128:(it + 1) * 128]
                                  .rearrange("(d p) n -> p d n", p=128)).then_inc(dsem, 16)
                    eng.dma_start(out=wsb[:, 16 * 128:32 * 128].rearrange("p (d n) -> p d n", d=16),
                                  in_=wu_t[:, it * 128:(it + 1) * 128]
                                  .rearrange("(d p) n -> p d n", p=128)).then_inc(dsem, 16)
                    return 2
                sch.task("gpsimd", t_ldw, writes=[wn], dma=2, tag=f"ldmlp{it}" + fx)

                def t_mm(eng, wsb=wsb):
                    for dt in range(DT):
                        eng.matmul(P0[:, 0:TH], wsb[:, dt * 128:(dt + 1) * 128],
                                   h2b[:, dt * TH:(dt + 1) * TH],
                                   start=(dt == 0), stop=(dt == DT - 1))
                    for dt in range(DT):
                        i = eng.matmul(P1[:, 0:TH], wsb[:, 2048 + dt * 128:2048 + (dt + 1) * 128],
                                       h2b[:, dt * TH:(dt + 1) * TH],
                                       start=(dt == 0), stop=(dt == DT - 1))
                    return i
                sch.task("tensor", t_mm, reads=[wn, "h2b"], writes=["P0", "P1"],
                         tag=f"mmmlp{it}" + fx)

                def t_silu(eng):
                    return eng.activation(out=tmp1[:], in_=P0[:, 0:TH], func=AF.Silu)
                sch.task("scalar", t_silu, reads=["P0"], writes=["tmp1"], tag=f"sil{it}" + fx)

                def t_ev(eng, it=it):
                    return eng.tensor_mul(out=big[:, it * TH:(it + 1) * TH],
                                          in0=tmp1[:], in1=P1[:, 0:TH])
                sch.task("vector", t_ev, reads=["tmp1", "P1"], writes=["big"], tag=f"evmlp{it}" + fx)

            for nt in range(16):
                wsb, wn = WB[nt % 2]
                def t_ldw(eng, dsem, nt=nt, wsb=wsb):
                    eng.dma_start(out=wsb[:, 0:40 * 128].rearrange("p (d n) -> p d n", d=40),
                                  in_=wd_t[:, nt * 128:(nt + 1) * 128]
                                  .rearrange("(d p) n -> p d n", p=128)).then_inc(dsem, 16)
                    return 1
                sch.task("gpsimd", t_ldw, writes=[wn], dma=1, tag=f"lddn{nt}" + fx)

                pb, pn = PS[nt % 4], psn[nt % 4]
                def t_mm(eng, wsb=wsb, pb=pb):
                    for it in range(NIT_MLP):
                        i = eng.matmul(pb[:, 0:TH], wsb[:, it * 128:(it + 1) * 128],
                                       big[:, it * TH:(it + 1) * TH],
                                       start=(it == 0), stop=(it == NIT_MLP - 1))
                    return i
                sch.task("tensor", t_mm, reads=[wn, "big"], writes=[pn], tag=f"mmdn{nt}" + fx)

                def t_ev(eng, nt=nt, pb=pb):
                    return eng.tensor_add(out=hmid[:, nt * TH:(nt + 1) * TH],
                                          in0=hmid[:, nt * TH:(nt + 1) * TH], in1=pb[:, 0:TH])
                sch.task("vector", t_ev, reads=[pn, "hmid"], writes=["hmid"], tag=f"evdn{nt}" + fx)

            if debug:
                def t_dbg2(eng, dsem, th=th):
                    eng.dma_start(out=dbg2_t[:, th * TH:(th + 1) * TH]
                                  .rearrange("(t p) n -> p t n", p=128),
                                  in_=hmid[:].rearrange("p (t n) -> p t n", t=DT)).then_inc(dsem, 16)
                    return 1
                sch.task("gpsimd", t_dbg2, reads=["hmid"], dma=1, tag="dbg2" + fx)

            ctx.stats_pipeline(ht, "hmid", DT, TH, sqb, "big", P4, "P4", P5, "P5",
                               "lnb0" + fx, use_mean=True)

            def t_x1(eng, xt=xt):
                for dt in range(DT):
                    eng.tensor_mul(out=tmp1[:], in0=hmid[:, dt * TH:(dt + 1) * TH], in1=P5[:, 0:TH])
                    eng.tensor_add(out=tmp1[:], in0=tmp1[:], in1=P4[:, 0:TH])
                    i = eng.tensor_add(out=xt(dt), in0=xt(dt), in1=tmp1[:])
                return i
            sch.task("vector", t_x1, reads=["hmid", "P4", "P5", "xb"],
                     writes=["xb", "tmp1"], tag="x1" + fx)

            if debug:
                def t_dbg3(eng, dsem, th=th):
                    eng.dma_start(out=dbg3_t[:, th * TH:(th + 1) * TH]
                                  .rearrange("(t p) n -> p t n", p=128),
                                  in_=xb[:].rearrange("p (t n) -> p t n", t=DT)
                                  [:, :, th * TH:(th + 1) * TH]).then_inc(dsem, 16)
                    return 1
                sch.task("gpsimd", t_dbg3, reads=["xb"], dma=1, tag="dbg3" + fx)

            ctx.stats_pipeline(xt, "xb", DT, TH, sqb, "big", P4, "P4", P5, "P5",
                               "lnx1" + fx, use_mean=True)

            def t_h3(eng, xt=xt):
                for dt in range(DT):
                    eng.tensor_mul(out=tmp1[:], in0=xt(dt), in1=P5[:, 0:TH])
                    i = eng.tensor_add(out=h2b[:, dt * TH:(dt + 1) * TH],
                                       in0=tmp1[:], in1=P4[:, 0:TH])
                return i
            sch.task("vector", t_h3, reads=["xb", "P4", "P5"], writes=["h2b", "tmp1"],
                     tag="h3" + fx)

            for it in range(NIT_FFN):
                wsb, wn = WB[it % 2]
                def t_ldw(eng, dsem, it=it, wsb=wsb):
                    eng.dma_start(out=wsb[:, 0:16 * 128].rearrange("p (d n) -> p d n", d=16),
                                  in_=w8g_t[:, it * 128:(it + 1) * 128]
                                  .rearrange("(d p) n -> p d n", p=128)).then_inc(dsem, 16)
                    eng.dma_start(out=wsb[:, 16 * 128:32 * 128].rearrange("p (d n) -> p d n", d=16),
                                  in_=w8u_t[:, it * 128:(it + 1) * 128]
                                  .rearrange("(d p) n -> p d n", p=128)).then_inc(dsem, 16)
                    return 2
                sch.task("gpsimd", t_ldw, writes=[wn], dma=2, tag=f"ldffn{it}" + fx)

                def t_mm(eng, wsb=wsb):
                    for dt in range(DT):
                        eng.matmul(P0[:, 0:TH], wsb[:, dt * 128:(dt + 1) * 128],
                                   h2b[:, dt * TH:(dt + 1) * TH],
                                   start=(dt == 0), stop=(dt == DT - 1))
                    for dt in range(DT):
                        i = eng.matmul(P1[:, 0:TH], wsb[:, 2048 + dt * 128:2048 + (dt + 1) * 128],
                                       h2b[:, dt * TH:(dt + 1) * TH],
                                       start=(dt == 0), stop=(dt == DT - 1))
                    return i
                sch.task("tensor", t_mm, reads=[wn, "h2b"], writes=["P0", "P1"],
                         tag=f"mmffn{it}" + fx)

                def t_sc(eng, it=it):
                    return eng.tensor_scalar_mul(out=tmp1[:], in0=P0[:, 0:TH],
                                                 scalar1=sgsb[:, it:it + 1])
                sch.task("vector", t_sc, reads=["P0", "sg"], writes=["tmp1"], tag=f"fsc{it}" + fx)

                def t_silu(eng):
                    return eng.activation(out=tmp1[:], in_=tmp1[:], func=AF.Silu)
                sch.task("scalar", t_silu, reads=["tmp1"], writes=["tmp1"], tag=f"fsi{it}" + fx)

                def t_ev(eng, it=it):
                    eng.tensor_scalar_mul(out=tmp2[:], in0=P1[:, 0:TH], scalar1=susb[:, it:it + 1])
                    return eng.tensor_mul(out=big[:, it * TH:(it + 1) * TH],
                                          in0=tmp1[:], in1=tmp2[:])
                sch.task("vector", t_ev, reads=["tmp1", "P1", "su"], writes=["big", "tmp2"],
                         tag=f"fev{it}" + fx)

            for nt in range(16):
                wsb, wn = WB[nt % 2]
                def t_ldw(eng, dsem, nt=nt, wsb=wsb):
                    eng.dma_start(out=wsb[:].rearrange("p (d n) -> p d n", d=64),
                                  in_=w8d_t[:, nt * 128:(nt + 1) * 128]
                                  .rearrange("(d p) n -> p d n", p=128)).then_inc(dsem, 16)
                    return 1
                sch.task("gpsimd", t_ldw, writes=[wn], dma=1, tag=f"ldfd{nt}" + fx)

                pb, pn = PS[nt % 4], psn[nt % 4]
                def t_mm(eng, wsb=wsb, pb=pb):
                    for it in range(NIT_FFN):
                        i = eng.matmul(pb[:, 0:TH], wsb[:, it * 128:(it + 1) * 128],
                                       big[:, it * TH:(it + 1) * TH],
                                       start=(it == 0), stop=(it == NIT_FFN - 1))
                    return i
                sch.task("tensor", t_mm, reads=[wn, "big"], writes=[pn], tag=f"mmfd{nt}" + fx)

                def t_ev(eng, nt=nt, pb=pb):
                    return eng.tensor_scalar_mul(out=hmid[:, nt * TH:(nt + 1) * TH],
                                                 in0=pb[:, 0:TH], scalar1=sdsb[:, nt:nt + 1])
                sch.task("vector", t_ev, reads=[pn, "sd"], writes=["hmid"], tag=f"evfd{nt}" + fx)

            if debug:
                def t_dbg4(eng, dsem, th=th):
                    eng.dma_start(out=dbg4_t[:, th * TH:(th + 1) * TH]
                                  .rearrange("(t p) n -> p t n", p=128),
                                  in_=hmid[:].rearrange("p (t n) -> p t n", t=DT)).then_inc(dsem, 16)
                    return 1
                sch.task("gpsimd", t_dbg4, reads=["hmid"], dma=1, tag="dbg4" + fx)

            ctx.stats_pipeline(ht, "hmid", DT, TH, sqb, "big", P4, "P4", P5, "P5",
                               "lnf" + fx, use_mean=True)

            for dt in range(DT):
                stg, sn = (stg0, "stg0") if dt % 2 == 0 else (stg1, "stg1")
                def t_x2(eng, dt=dt, stg=stg, xt=xt):
                    eng.tensor_mul(out=tmp1[:], in0=hmid[:, dt * TH:(dt + 1) * TH], in1=P5[:, 0:TH])
                    eng.tensor_add(out=tmp1[:], in0=tmp1[:], in1=P4[:, 0:TH])
                    return eng.tensor_add(out=stg[:], in0=xt(dt), in1=tmp1[:])
                sch.task("vector", t_x2, reads=["hmid", "P4", "P5", "xb"],
                         writes=[sn, "tmp1"], tag=f"x2{dt}" + fx)

                def t_st(eng, dsem, dt=dt, stg=stg, th=th):
                    eng.dma_start(out=x2_t[dt * 128:(dt + 1) * 128, th * TH:(th + 1) * TH],
                                  in_=stg[:]).then_inc(dsem, 16)
                    return 1
                sch.task("sync", t_st, reads=[sn], dma=1, tag=f"st{dt}" + fx)

        @block.sync
        def _(eng):
            sch.run("sync", eng)

        @block.gpsimd
        def _(eng):
            sch.run("gpsimd", eng)

        @block.tensor
        def _(eng):
            sch.run("tensor", eng)

        @block.vector
        def _(eng):
            sch.run("vector", eng)

        @block.scalar
        def _(eng):
            sch.run("scalar", eng)
    return nc


def run_launch_B(inp, gatedT, debug=False):
    per_core = stage_launch_B(inp, gatedT)
    nc = build_launch_B(debug=debug)
    res = run_bass_kernel_spmd(nc, per_core, core_ids=list(range(NCORE)))
    x2 = np.concatenate([res.results[c]["x2"] for c in range(NCORE)], axis=1)
    if debug:
        dbg = {k: np.concatenate([res.results[c][k] for c in range(NCORE)], axis=1)
               for k in ("dbg1", "dbg2", "dbg3", "dbg4")}
        return x2, dbg
    return x2


def stage_launch_C(inp, x2):
    Wq = _f32(inp["L1_attn_Wq"])
    Wk = _f32(inp["L1_attn_Wk"])
    Wv = _f32(inp["L1_attn_Wv"])
    Wo = _f32(inp["L1_attn_Wo"])
    lam = _f32(np.asarray(inp["L1_attn_lam"]).reshape(1, 16))
    T = 1024
    HD = 128
    SUB = 64
    theta = 10000.0
    inv_freq = 1.0 / theta ** (np.arange(0, HD, 2, dtype=np.float32) / HD)
    per_core = []
    for c in range(NCORE):
        g0 = c * T
        xsl = np.zeros((D, T + 512), np.float32)
        lo = max(0, g0 - 512)
        xsl[:, 512 - (g0 - lo):] = x2[:, lo:g0 + T]
        pos = np.arange(g0 - 512, g0 + T, dtype=np.float32)
        pos = np.maximum(pos, 0.0)
        fr = pos[None, :] * inv_freq[:, None]
        mk = np.zeros((8, 5, 128, 128), np.float32)
        rows = np.arange(128)
        cols = np.arange(128)
        for qt in range(8):
            qglob = g0 + qt * 128 + rows[:, None]
            for j in range(5):
                kglob = g0 - 512 + (qt + j) * 128 + cols[None, :]
                ok = (kglob >= 0) & (kglob <= qglob) & (kglob >= qglob - 511)
                mk[qt, j] = np.where(ok, 0.0, NEG)
        mkf = mk.transpose(2, 0, 1, 3).reshape(128, 8 * 5 * 128)
        per_core.append({
            "xsl": _f32(xsl),
            "wq": Wq, "wk": Wk, "wv": Wv, "wo": Wo, "lam": lam,
            "ropec": _f32(np.cos(fr)), "ropes": _f32(np.sin(fr)),
            "masks": _f32(mkf),
            "ident": np.eye(128, dtype=np.float32),
        })
    return per_core


def build_launch_C(debug=False):
    nc = bass.Bass("TRN2", target_bir_lowering=False)
    T = 1024
    TH = 512
    W = T + 512
    xsl_t = nc.dram_tensor("xsl", [D, W], F32, kind="ExternalInput").ap()
    wq_t = nc.dram_tensor("wq", [D, 2048], F32, kind="ExternalInput").ap()
    wk_t = nc.dram_tensor("wk", [D, 512], F32, kind="ExternalInput").ap()
    wv_t = nc.dram_tensor("wv", [D, 512], F32, kind="ExternalInput").ap()
    wo_t = nc.dram_tensor("wo", [2048, D], F32, kind="ExternalInput").ap()
    lam_t = nc.dram_tensor("lam", [1, 16], F32, kind="ExternalInput").ap()
    ropec_t = nc.dram_tensor("ropec", [64, W], F32, kind="ExternalInput").ap()
    ropes_t = nc.dram_tensor("ropes", [64, W], F32, kind="ExternalInput").ap()
    masks_t = nc.dram_tensor("masks", [128, 8 * 5 * 128], F32, kind="ExternalInput").ap()
    ident_t = nc.dram_tensor("ident", [128, 128], F32, kind="ExternalInput").ap()
    x3_t = nc.dram_tensor("x3", [D, T], F32, kind="ExternalOutput").ap()
    if debug:
        dbgy_t = nc.dram_tensor("dbgy", [D, W], F32, kind="ExternalOutput").ap()
        dbgq_t = nc.dram_tensor("dbgq", [2048, T], F32, kind="ExternalOutput").ap()
        dbgk_t = nc.dram_tensor("dbgk", [512, W], F32, kind="ExternalOutput").ap()
        dbgv_t = nc.dram_tensor("dbgv", [512, W], F32, kind="ExternalOutput").ap()
        dbga_t = nc.dram_tensor("dbga", [2048, T], F32, kind="ExternalOutput").ap()
        dbgs_t = nc.dram_tensor("dbgs", [128, 4 * 640 + 16], F32, kind="ExternalOutput").ap()

    SCALE = float(1.0 / np.sqrt(64.0))

    from contextlib import ExitStack
    with ExitStack() as st:
        chunkb = st.enter_context(nc.sbuf_tensor("chunkb", [128, DT * TH], BF16))
        y3b = st.enter_context(nc.sbuf_tensor("y3b", [128, DT * W], BF16))
        qb = st.enter_context(nc.sbuf_tensor("qb", [128, 16 * T], BF16))
        kb = st.enter_context(nc.sbuf_tensor("kb", [128, 4 * W], BF16))
        vb = st.enter_context(nc.sbuf_tensor("vb", [128, 4 * W], BF16))
        vtok = st.enter_context(nc.sbuf_tensor("vtok", [128, 4 * 12 * 128], BF16))
        sqb = st.enter_context(nc.sbuf_tensor("sqb", [128, DT * TH], BF16))
        ropec = st.enter_context(nc.sbuf_tensor("ropec_sb", [64, W], F32))
        ropes = st.enter_context(nc.sbuf_tensor("ropes_sb", [64, W], F32))
        maskb = st.enter_context(nc.sbuf_tensor("maskb", [128, 5 * 128], BF16))
        lamb = st.enter_context(nc.sbuf_tensor("lamb", [128, 16], F32))
        rtmp1 = st.enter_context(nc.sbuf_tensor("rtmp1", [64, W], BF16))
        rtmp2 = st.enter_context(nc.sbuf_tensor("rtmp2", [64, W], BF16))
        rtmp3 = st.enter_context(nc.sbuf_tensor("rtmp3", [64, W], BF16))
        s1sb = st.enter_context(nc.sbuf_tensor("s1sb", [128, 640], F32))
        s2sb = st.enter_context(nc.sbuf_tensor("s2sb", [128, 640], F32))
        diffb = st.enter_context(nc.sbuf_tensor("diffb", [128, 640], BF16))
        red = st.enter_context(nc.sbuf_tensor("red", [128, 8], F32))
        outtok = st.enter_context(nc.sbuf_tensor("outtok", [128, 128], BF16))
        ident_b = st.enter_context(nc.sbuf_tensor("ident_b", [128, 128], BF16))
        ident_f = st.enter_context(nc.sbuf_tensor("ident_f", [128, 128], F32))
        wb0 = st.enter_context(nc.sbuf_tensor("wb0", [128, 16 * 128], BF16))
        wb1 = st.enter_context(nc.sbuf_tensor("wb1", [128, 16 * 128], BF16))
        stg0 = st.enter_context(nc.sbuf_tensor("stg0", [128, TH], F32))
        stg1 = st.enter_context(nc.sbuf_tensor("stg1", [128, TH], F32))
        tmp1 = st.enter_context(nc.sbuf_tensor("tmp1", [128, TH], BF16))
        P0 = st.enter_context(nc.psum_tensor("P0", [128, 512], F32))
        P1 = st.enter_context(nc.psum_tensor("P1", [128, 512], F32))
        P2 = st.enter_context(nc.psum_tensor("P2", [128, 512], F32))
        P3 = st.enter_context(nc.psum_tensor("P3", [128, 512], F32))
        P4 = st.enter_context(nc.psum_tensor("P4", [128, 512], F32))
        P5 = st.enter_context(nc.psum_tensor("P5", [128, 512], F32))
        P6 = st.enter_context(nc.psum_tensor("P6", [128, 512], F32))
        PB7 = st.enter_context(nc.psum_tensor("PB7", [128, 1024], BF16))
        block = st.enter_context(nc.Block())
        sch = Sched(nc)
        ctx = Ctx(nc, sch, st)
        ctx.init_tasks()
        PS = [P0, P1, P2, P3]
        psn = ["P0", "P1", "P2", "P3"]
        WB = [(wb0, "wb0"), (wb1, "wb1")]

        def t_pre(eng, dsem):
            eng.dma_start(out=ropec[:], in_=ropec_t).then_inc(dsem, 16)
            eng.dma_start(out=ropes[:], in_=ropes_t).then_inc(dsem, 16)
            eng.dma_start(out=ident_f[:], in_=ident_t).then_inc(dsem, 16)
            return 3
        sch.task("sync", t_pre, writes=["ropec", "ropes", "ident_f"], dma=3, tag="preC")

        def t_preb(eng):
            return eng.tensor_copy(out=ident_b[:], in_=ident_f[:])
        sch.task("vector", t_preb, reads=["ident_f"], writes=["ident_b"], tag="prebC")

        def t_lam(eng, dsem):
            eng.dma_start(out=lamb[0:1, :], in_=lam_t).then_inc(dsem, 16)
            return 1
        sch.task("gpsimd", t_lam, writes=["lamr"], dma=1, tag="lamC")

        def t_lam_mm(eng):
            return eng.matmul(P6[:, 0:16], ctx.ones1f[0:1, :], lamb[0:1, :], start=True, stop=True)
        sch.task("tensor", t_lam_mm, reads=["lamr", "o1f"], writes=["P6"], tag="lamMM")

        def t_lam_ev(eng):
            return eng.tensor_copy(out=lamb[:], in_=P6[:, 0:16])
        sch.task("vector", t_lam_ev, reads=["P6"], writes=["lamb"], tag="lamEV")

        for ch in range(3):
            fx = f"c{ch}"
            def t_ldx(eng, dsem, ch=ch):
                eng.dma_start(out=chunkb[:].rearrange("p (t n) -> p t n", t=DT),
                              in_=xsl_t[:, ch * TH:(ch + 1) * TH]
                              .rearrange("(t p) n -> p t n", p=128)).then_inc(dsem, 16)
                return 1
            sch.task("gpsimd", t_ldx, writes=["chunkb"], dma=1, tag="ldx" + fx)

            ct = lambda dt: chunkb[:, dt * TH:(dt + 1) * TH]
            ctx.stats_pipeline(ct, "chunkb", DT, TH, sqb, "sqb", P4, "P4", P5, "P5",
                               "lnx2" + fx, use_mean=True)

            def t_y3(eng, ch=ch, ct=ct):
                for dt in range(DT):
                    eng.tensor_mul(out=tmp1[:], in0=ct(dt), in1=P5[:, 0:TH])
                    i = eng.tensor_add(out=y3b[:, dt * W + ch * TH: dt * W + (ch + 1) * TH],
                                       in0=tmp1[:], in1=P4[:, 0:TH])
                return i
            sch.task("vector", t_y3, reads=["chunkb", "P4", "P5"], writes=["y3b", "tmp1"],
                     tag="y3" + fx)

        if debug:
            def t_dbgy(eng, dsem):
                eng.dma_start(out=dbgy_t.rearrange("(t p) n -> p t n", p=128),
                              in_=y3b[:].rearrange("p (t n) -> p t n", t=DT)).then_inc(dsem, 16)
                return 1
            sch.task("gpsimd", t_dbgy, reads=["y3b"], dma=1, tag="dbgy")

        for nt in range(16):
            wsb, wn = WB[nt % 2]
            def t_ldw(eng, dsem, nt=nt, wsb=wsb):
                eng.dma_start(out=wsb[:].rearrange("p (d n) -> p d n", d=16),
                              in_=wq_t[:, nt * 128:(nt + 1) * 128]
                              .rearrange("(d p) n -> p d n", p=128)).then_inc(dsem, 16)
                return 1
            sch.task("gpsimd", t_ldw, writes=[wn], dma=1, tag=f"ldwq{nt}")

            for th in range(2):
                pb, pn = PS[(2 * nt + th) % 4], psn[(2 * nt + th) % 4]
                def t_mm(eng, wsb=wsb, pb=pb, th=th):
                    for dt in range(DT):
                        i = eng.matmul(pb[:, 0:TH], wsb[:, dt * 128:(dt + 1) * 128],
                                       y3b[:, dt * W + 512 + th * TH: dt * W + 512 + (th + 1) * TH],
                                       start=(dt == 0), stop=(dt == DT - 1))
                    return i
                sch.task("tensor", t_mm, reads=[wn, "y3b"], writes=[pn], tag=f"mmq{nt}{th}")

                def t_ev(eng, nt=nt, th=th, pb=pb):
                    return eng.tensor_copy(out=qb[:, nt * T + th * TH: nt * T + (th + 1) * TH],
                                           in_=pb[:, 0:TH])
                sch.task("vector", t_ev, reads=[pn], writes=[f"qb{nt}"], tag=f"evq{nt}{th}")

        for which, wt, dst, dname in (("k", wk_t, kb, "kb"), ("v", wv_t, vb, "vb")):
            for nt in range(4):
                wsb, wn = WB[nt % 2]
                def t_ldw(eng, dsem, nt=nt, wsb=wsb, wt=wt):
                    eng.dma_start(out=wsb[:].rearrange("p (d n) -> p d n", d=16),
                                  in_=wt[:, nt * 128:(nt + 1) * 128]
                                  .rearrange("(d p) n -> p d n", p=128)).then_inc(dsem, 16)
                    return 1
                sch.task("gpsimd", t_ldw, writes=[wn], dma=1, tag=f"ldw{which}{nt}")

                for ch in range(3):
                    pb, pn = PS[ch % 4], psn[ch % 4]
                    def t_mm(eng, wsb=wsb, pb=pb, ch=ch):
                        for dt in range(DT):
                            i = eng.matmul(pb[:, 0:TH], wsb[:, dt * 128:(dt + 1) * 128],
                                           y3b[:, dt * W + ch * TH: dt * W + (ch + 1) * TH],
                                           start=(dt == 0), stop=(dt == DT - 1))
                        return i
                    sch.task("tensor", t_mm, reads=[wn, "y3b"], writes=[pn],
                             tag=f"mm{which}{nt}{ch}")

                    def t_ev(eng, nt=nt, ch=ch, pb=pb, dst=dst):
                        return eng.tensor_copy(
                            out=dst[:, nt * W + ch * TH: nt * W + (ch + 1) * TH],
                            in_=pb[:, 0:TH])
                    sch.task("vector", t_ev, reads=[pn], writes=[f"{dname}{nt}"],
                             tag=f"ev{which}{nt}{ch}")

        def emit_rope(buf, rname, nheads, width, coff, tagp):
            for h in range(nheads):
                def t_rope(eng, buf=buf, h=h, width=width, coff=coff):
                    b0 = h * width
                    q1 = buf[0:64, b0:b0 + width]
                    q2 = buf[64:128, b0:b0 + width]
                    cosv = ropec[:, coff:coff + width]
                    sinv = ropes[:, coff:coff + width]
                    eng.tensor_copy(out=rtmp2[0:64, 0:width], in_=q2)
                    eng.tensor_mul(out=rtmp1[0:64, 0:width], in0=q1, in1=cosv)
                    eng.tensor_mul(out=rtmp3[0:64, 0:width], in0=rtmp2[0:64, 0:width], in1=sinv)
                    eng.tensor_sub(out=rtmp1[0:64, 0:width], in0=rtmp1[0:64, 0:width],
                                   in1=rtmp3[0:64, 0:width])
                    eng.tensor_mul(out=rtmp2[0:64, 0:width], in0=rtmp2[0:64, 0:width], in1=cosv)
                    eng.tensor_mul(out=rtmp3[0:64, 0:width], in0=q1, in1=sinv)
                    eng.tensor_add(out=rtmp2[0:64, 0:width], in0=rtmp2[0:64, 0:width],
                                   in1=rtmp3[0:64, 0:width])
                    eng.tensor_copy(out=q2, in_=rtmp2[0:64, 0:width])
                    return eng.tensor_copy(out=q1, in_=rtmp1[0:64, 0:width])
                sch.task("vector", t_rope, reads=[f"{rname}{h}", "ropec", "ropes"],
                         writes=[f"{rname}{h}", "rtmp1", "rtmp2", "rtmp3"], tag=f"rope{tagp}{h}")
        emit_rope(qb, "qb", 16, T, 512, "q")
        emit_rope(kb, "kb", 4, W, 0, "k")

        for h in range(4):
            for jt in range(6):
                def t_vT(eng, h=h, jt=jt):
                    eng.transpose(PB7[:, 0:128], vb[:, h * W + (2 * jt) * 128: h * W + (2 * jt + 1) * 128], ident_b[:])
                    return eng.transpose(PB7[:, 128:256], vb[:, h * W + (2 * jt + 1) * 128: h * W + (2 * jt + 2) * 128], ident_b[:])
                sch.task("tensor", t_vT, reads=[f"vb{h}", "ident_b"], writes=["PB7"],
                         tag=f"vT{h}{jt}")

                def t_vTe(eng, h=h, jt=jt):
                    return eng.tensor_copy(
                        out=vtok[:, (h * 12 + 2 * jt) * 128:(h * 12 + 2 * jt + 2) * 128],
                        in_=PB7[:, 0:256])
                sch.task("vector", t_vTe, reads=["PB7"], writes=["vtok"], tag=f"vTe{h}{jt}")

        if debug:
            def t_dbgq(eng, dsem):
                eng.dma_start(out=dbgq_t.rearrange("(t p) n -> p t n", p=128),
                              in_=qb[:].rearrange("p (t n) -> p t n", t=16)).then_inc(dsem, 16)
                eng.dma_start(out=dbgk_t.rearrange("(t p) n -> p t n", p=128),
                              in_=kb[:].rearrange("p (t n) -> p t n", t=4)).then_inc(dsem, 16)
                eng.dma_start(out=dbgv_t.rearrange("(t p) n -> p t n", p=128),
                              in_=vb[:].rearrange("p (t n) -> p t n", t=4)).then_inc(dsem, 16)
                return 3
            sch.task("gpsimd", t_dbgq,
                     reads=[f"qb{h}" for h in range(16)] + [f"kb{h}" for h in range(4)]
                           + [f"vb{h}" for h in range(4)],
                     dma=3, tag="dbgq")

        for qt in range(8):
            def t_ldm(eng, dsem, qt=qt):
                eng.dma_start(out=maskb[:], in_=masks_t[:, qt * 640:(qt + 1) * 640]).then_inc(dsem, 16)
                return 1
            sch.task("gpsimd", t_ldm, writes=["maskb"], dma=1, tag=f"ldm{qt}")

            for h in range(16):
                hk = h // 4
                fx = f"u{qt}_{h}"
                qoff = h * T + qt * 128
                koff = hk * W + qt * 128

                def t_s(eng, qoff=qoff, koff=koff):
                    for j in range(4):
                        eng.matmul(P0[:, j * 128:(j + 1) * 128], qb[0:64, qoff:qoff + 128],
                                   kb[0:64, koff + j * 128: koff + (j + 1) * 128],
                                   start=True, stop=True)
                        eng.matmul(P2[:, j * 128:(j + 1) * 128], qb[64:128, qoff:qoff + 128],
                                   kb[64:128, koff + j * 128: koff + (j + 1) * 128],
                                   start=True, stop=True)
                    eng.matmul(P1[:, 0:128], qb[0:64, qoff:qoff + 128],
                               kb[0:64, koff + 512: koff + 640], start=True, stop=True)
                    return eng.matmul(P3[:, 0:128], qb[64:128, qoff:qoff + 128],
                                      kb[64:128, koff + 512: koff + 640], start=True, stop=True)
                sch.task("tensor", t_s, reads=[f"qb{h}", f"kb{hk}"],
                         writes=["P0", "P1", "P2", "P3"], tag="s" + fx)

                def t_sev(eng):
                    eng.scalar_tensor_tensor(out=s1sb[:, 0:512], in0=P0[:, 0:512], scalar=SCALE,
                                             in1=maskb[:, 0:512], op0=ALU.mult, op1=ALU.add)
                    eng.scalar_tensor_tensor(out=s1sb[:, 512:640], in0=P1[:, 0:128], scalar=SCALE,
                                             in1=maskb[:, 512:640], op0=ALU.mult, op1=ALU.add)
                    eng.scalar_tensor_tensor(out=s2sb[:, 0:512], in0=P2[:, 0:512], scalar=SCALE,
                                             in1=maskb[:, 0:512], op0=ALU.mult, op1=ALU.add)
                    return eng.scalar_tensor_tensor(out=s2sb[:, 512:640], in0=P3[:, 0:128],
                                                    scalar=SCALE, in1=maskb[:, 512:640],
                                                    op0=ALU.mult, op1=ALU.add)
                sch.task("vector", t_sev, reads=["P0", "P1", "P2", "P3", "maskb"],
                         writes=["s1sb", "s2sb"], tag="sev" + fx)

                if debug and qt == 1 and h == 0:
                    def t_ds1(eng, dsem):
                        eng.dma_start(out=dbgs_t[:, 0:640], in_=s1sb[:]).then_inc(dsem, 16)
                        eng.dma_start(out=dbgs_t[:, 640:1280], in_=s2sb[:]).then_inc(dsem, 16)
                        return 2
                    sch.task("gpsimd", t_ds1, reads=["s1sb", "s2sb"], dma=2, tag="ds1")

                def t_max(eng):
                    eng.reduce_max(out=red[:, 0:1], in_=s1sb[:], axis=AX.X)
                    eng.reduce_max(out=red[:, 1:2], in_=s2sb[:], axis=AX.X)
                    sch.bar(eng)
                    return eng.tensor_scalar_mul(out=red[:, 0:2], in0=red[:, 0:2], scalar1=-1.0)
                sch.task("vector", t_max, reads=["s1sb", "s2sb"], writes=["red"], tag="mx" + fx)

                def t_exp(eng):
                    eng.activation(out=s1sb[:], in_=s1sb[:], func=AF.Exp, bias=red[:, 0:1])
                    return eng.activation(out=s2sb[:], in_=s2sb[:], func=AF.Exp, bias=red[:, 1:2])
                sch.task("scalar", t_exp, reads=["s1sb", "s2sb", "red"],
                         writes=["s1sb", "s2sb"], tag="ex" + fx)

                if debug and qt == 1 and h == 0:
                    def t_ds0(eng, dsem):
                        eng.dma_start(out=dbgs_t[:, 640:1280], in_=s1sb[:]).then_inc(dsem, 16)
                        return 1
                    sch.task("gpsimd", t_ds0, reads=["s1sb"], dma=1, tag="ds0")

                def t_sum(eng, h=h):
                    eng.reduce_sum(out=red[:, 2:3], in_=s1sb[:], axis=AX.X)
                    eng.reduce_sum(out=red[:, 3:4], in_=s2sb[:], axis=AX.X)
                    sch.bar(eng)
                    eng.reciprocal(out=red[:, 4:5], in_=red[:, 2:3])
                    eng.reciprocal(out=red[:, 5:6], in_=red[:, 3:4])
                    sch.bar(eng)
                    eng.tensor_mul(out=red[:, 6:7], in0=red[:, 5:6], in1=lamb[:, h:h + 1])
                    sch.bar(eng)
                    return eng.tensor_scalar_mul(out=red[:, 6:7], in0=red[:, 6:7], scalar1=-1.0)
                sch.task("vector", t_sum, reads=["s1sb", "s2sb", "lamb"], writes=["red"],
                         tag="sm" + fx)

                def t_diff(eng):
                    eng.tensor_scalar_mul(out=s1sb[:], in0=s1sb[:], scalar1=red[:, 4:5])
                    eng.scalar_tensor_tensor(out=s1sb[:], in0=s2sb[:], scalar=red[:, 6:7],
                                             in1=s1sb[:], op0=ALU.mult, op1=ALU.add)
                    eng.tensor_scalar_max(out=s1sb[:], in0=s1sb[:], scalar1=0.0)
                    eng.reduce_sum(out=red[:, 7:8], in_=s1sb[:], axis=AX.X)
                    sch.bar(eng)
                    eng.tensor_scalar_add(out=red[:, 7:8], in0=red[:, 7:8], scalar1=1e-6)
                    sch.bar(eng)
                    eng.reciprocal(out=red[:, 7:8], in_=red[:, 7:8])
                    sch.bar(eng)
                    return eng.tensor_scalar_mul(out=diffb[:], in0=s1sb[:], scalar1=red[:, 7:8])
                sch.task("vector", t_diff, reads=["s1sb", "s2sb", "red"],
                         writes=["diffb", "s1sb", "red"], tag="df" + fx)

                if debug and qt == 1 and h == 0:
                    def t_ds2(eng, dsem):
                        eng.dma_start(out=dbgs_t[:, 1280:1920], in_=s1sb[:]).then_inc(dsem, 16)
                        eng.dma_start(out=dbgs_t[:, 1920:2552],
                                      in_=diffb[:, 0:632]).then_inc(dsem, 16)
                        eng.dma_start(out=dbgs_t[:, 2552:2560], in_=red[:]).then_inc(dsem, 16)
                        eng.dma_start(out=dbgs_t[:, 2560:2576], in_=lamb[:]).then_inc(dsem, 16)
                        return 4
                    sch.task("gpsimd", t_ds2, reads=["s1sb", "diffb", "red", "lamb"], dma=4, tag="ds2")

                def t_dT(eng):
                    for j in range(5):
                        i = eng.transpose(PB7[:, j * 128:(j + 1) * 128],
                                          diffb[:, j * 128:(j + 1) * 128], ident_b[:])
                    return i
                sch.task("tensor", t_dT, reads=["diffb", "ident_b"], writes=["PB7"], tag="dT" + fx)

                def t_dTe(eng):
                    return eng.tensor_copy(out=diffb[:], in_=PB7[:, 0:640])
                sch.task("vector", t_dTe, reads=["PB7"], writes=["diffb"], tag="dTe" + fx)

                def t_pv(eng, hk=hk, qt=qt):
                    for j in range(5):
                        i = eng.matmul(P4[:, 0:128], diffb[:, j * 128:(j + 1) * 128],
                                       vtok[:, (hk * 12 + qt + j) * 128:(hk * 12 + qt + j + 1) * 128],
                                       start=(j == 0), stop=(j == 4))
                    return i
                sch.task("tensor", t_pv, reads=["diffb", "vtok"], writes=["P4"], tag="pv" + fx)

                def t_pve(eng):
                    return eng.tensor_copy(out=outtok[:], in_=P4[:, 0:128])
                sch.task("vector", t_pve, reads=["P4"], writes=["outtok"], tag="pve" + fx)

                def t_oT(eng):
                    return eng.transpose(PB7[:, 0:128], outtok[:], ident_b[:])
                sch.task("tensor", t_oT, reads=["outtok", "ident_b"], writes=["PB7"], tag="oT" + fx)

                def t_oTe(eng, h=h, qt=qt):
                    return eng.tensor_copy(out=qb[:, h * T + qt * 128:h * T + (qt + 1) * 128],
                                           in_=PB7[:, 0:128])
                sch.task("vector", t_oTe, reads=["PB7"], writes=[f"qb{h}"], tag="oTe" + fx)

        if debug:
            def t_dbga(eng, dsem):
                eng.dma_start(out=dbga_t.rearrange("(t p) n -> p t n", p=128),
                              in_=qb[:].rearrange("p (t n) -> p t n", t=16)).then_inc(dsem, 16)
                return 1
            sch.task("gpsimd", t_dbga, reads=[f"qb{h}" for h in range(16)], dma=1, tag="dbga")

        for th in range(2):
            fx = f"o{th}"
            for nt in range(16):
                wsb, wn = WB[nt % 2]
                def t_ldw(eng, dsem, nt=nt, wsb=wsb):
                    eng.dma_start(out=wsb[:].rearrange("p (d n) -> p d n", d=16),
                                  in_=wo_t[:, nt * 128:(nt + 1) * 128]
                                  .rearrange("(d p) n -> p d n", p=128)).then_inc(dsem, 16)
                    return 1
                sch.task("gpsimd", t_ldw, writes=[wn], dma=1, tag=f"ldwo{nt}" + fx)

                pb, pn = PS[nt % 4], psn[nt % 4]
                def t_mm(eng, wsb=wsb, pb=pb, th=th):
                    for dt in range(DT):
                        i = eng.matmul(pb[:, 0:TH], wsb[:, dt * 128:(dt + 1) * 128],
                                       qb[:, dt * T + th * TH: dt * T + (th + 1) * TH],
                                       start=(dt == 0), stop=(dt == DT - 1))
                    return i
                sch.task("tensor", t_mm,
                         reads=[wn] + [f"qb{hh}" for hh in range(16)],
                         writes=[pn], tag=f"mmo{nt}" + fx)

                def t_ev(eng, nt=nt, pb=pb):
                    return eng.tensor_copy(out=chunkb[:, nt * TH:(nt + 1) * TH], in_=pb[:, 0:TH])
                sch.task("vector", t_ev, reads=[pn], writes=["chunkb"], tag=f"evo{nt}" + fx)

            ct = lambda dt: chunkb[:, dt * TH:(dt + 1) * TH]
            ctx.stats_pipeline(ct, "chunkb", DT, TH, sqb, "sqb", P4, "P4", P5, "P5",
                               "lnat" + fx, use_mean=True)

            for dt in range(DT):
                stg, sn = (stg0, "stg0") if dt % 2 == 0 else (stg1, "stg1")
                def t_ldx2(eng, dsem, dt=dt, th=th, stg=stg):
                    eng.dma_start(out=stg[:],
                                  in_=xsl_t[dt * 128:(dt + 1) * 128,
                                            512 + th * TH: 512 + (th + 1) * TH]).then_inc(dsem, 16)
                    return 1
                sch.task("sync", t_ldx2, writes=[sn], dma=1, tag=f"ldx2{dt}" + fx)

                def t_x3(eng, dt=dt, stg=stg, ct=ct):
                    eng.tensor_mul(out=tmp1[:], in0=ct(dt), in1=P5[:, 0:TH])
                    eng.tensor_add(out=tmp1[:], in0=tmp1[:], in1=P4[:, 0:TH])
                    return eng.tensor_add(out=stg[:], in0=stg[:], in1=tmp1[:])
                sch.task("vector", t_x3, reads=["chunkb", "P4", "P5", sn],
                         writes=[sn, "tmp1"], tag=f"x3{dt}" + fx)

                def t_st(eng, dsem, dt=dt, th=th, stg=stg):
                    eng.dma_start(out=x3_t[dt * 128:(dt + 1) * 128, th * TH:(th + 1) * TH],
                                  in_=stg[:]).then_inc(dsem, 16)
                    return 1
                sch.task("sync", t_st, reads=[sn], dma=1, tag=f"stx3{dt}" + fx)

        @block.sync
        def _(eng):
            sch.run("sync", eng)

        @block.gpsimd
        def _(eng):
            sch.run("gpsimd", eng)

        @block.tensor
        def _(eng):
            sch.run("tensor", eng)

        @block.vector
        def _(eng):
            sch.run("vector", eng)

        @block.scalar
        def _(eng):
            sch.run("scalar", eng)
    return nc


def run_launch_C(inp, x2, debug=False):
    per_core = stage_launch_C(inp, x2)
    nc = build_launch_C(debug=debug)
    res = run_bass_kernel_spmd(nc, per_core, core_ids=list(range(NCORE)))
    x3 = np.concatenate([res.results[c]["x3"] for c in range(NCORE)], axis=1)
    if debug:
        dbg = {k: [res.results[c][k] for c in range(NCORE)]
               for k in ("dbgy", "dbgq", "dbgk", "dbgv", "dbga", "dbgs")}
        return x3, dbg
    return x3


def _repack_kmajor(Wp, n_slice=None):
    Wp = np.asarray(Wp).astype(np.uint32)
    N, K2 = Wp.shape
    K = K2 * 2
    lo = Wp & 15
    hi = (Wp >> 4) & 15
    nib = np.empty((N, K), np.uint32)
    nib[:, 0::2] = lo
    nib[:, 1::2] = hi
    nib = nib.T
    out = np.zeros((K, N // 8), np.uint32)
    for j in range(8):
        out |= nib[:, j::8] << (4 * j)
    return out.astype(np.int32)


def stage_launch_D(inp):
    gP = _repack_kmajor(inp["L1_ffn_gate_W_packed"])
    uP = _repack_kmajor(inp["L1_ffn_up_W_packed"])
    dP = _repack_kmajor(inp["L1_ffn_down_W_packed"])
    gS = _f32(np.asarray(inp["L1_ffn_gate_scales"]).T)
    uS = _f32(np.asarray(inp["L1_ffn_up_scales"]).T)
    dS = _f32(np.asarray(inp["L1_ffn_down_scales"]).T)
    selK = np.zeros((4, 128), np.float32)
    for r in range(4):
        selK[r, r * 32:(r + 1) * 32] = 1.0
    per_core = []
    for c in range(NCORE):
        nsl = slice(c * 2048 // 8 * 8, (c + 1) * 2048)
        ns = slice(c * 2048, (c + 1) * 2048)
        per_core.append({
            "gp": np.ascontiguousarray(gP[:, c * 256:(c + 1) * 256]),
            "up": np.ascontiguousarray(uP[:, c * 256:(c + 1) * 256]),
            "dp": np.ascontiguousarray(dP[c * 2048:(c + 1) * 2048, :]),
            "gsc": np.ascontiguousarray(gS[:, ns]),
            "usc": np.ascontiguousarray(uS[:, ns]),
            "dsc": np.ascontiguousarray(dS[c * 64:(c + 1) * 64, :]),
            "selk": selK,
        })
    return per_core


def build_launch_D():
    nc = bass.Bass("TRN2", target_bir_lowering=False)
    gp_t = nc.dram_tensor("gp", [2048, 256], I32, kind="ExternalInput").ap()
    up_t = nc.dram_tensor("up", [2048, 256], I32, kind="ExternalInput").ap()
    dp_t = nc.dram_tensor("dp", [2048, 256], I32, kind="ExternalInput").ap()
    gsc_t = nc.dram_tensor("gsc", [64, 2048], F32, kind="ExternalInput").ap()
    usc_t = nc.dram_tensor("usc", [64, 2048], F32, kind="ExternalInput").ap()
    dsc_t = nc.dram_tensor("dsc", [64, 2048], F32, kind="ExternalInput").ap()
    selk_t = nc.dram_tensor("selk", [4, 128], F32, kind="ExternalInput").ap()
    wgo_t = nc.dram_tensor("wgo", [2048, 2048], BF16, kind="ExternalOutput").ap()
    wuo_t = nc.dram_tensor("wuo", [2048, 2048], BF16, kind="ExternalOutput").ap()
    wdo_t = nc.dram_tensor("wdo", [2048, 2048], BF16, kind="ExternalOutput").ap()

    from contextlib import ExitStack
    with ExitStack() as st:
        pk = st.enter_context(nc.sbuf_tensor("pk", [128, 256], I32))
        nib = st.enter_context(nc.sbuf_tensor("nib", [128, 256], I32))
        nf = st.enter_context(nc.sbuf_tensor("nf", [128, 2048], F32))
        sgn = st.enter_context(nc.sbuf_tensor("sgn", [128, 2048], F32))
        mp = st.enter_context(nc.sbuf_tensor("mp", [128, 2048], F32))
        r4 = st.enter_context(nc.sbuf_tensor("r4", [128, 2048], F32))
        r6 = st.enter_context(nc.sbuf_tensor("r6", [128, 2048], F32))
        scrow = st.enter_context(nc.sbuf_tensor("scrow", [4, 2048], F32))
        scb = st.enter_context(nc.sbuf_tensor("scb", [128, 2048], F32))
        selk = st.enter_context(nc.sbuf_tensor("selk_sb", [4, 128], F32))
        wout = st.enter_context(nc.sbuf_tensor("wout", [128, 2048], BF16))
        cst = st.enter_context(nc.sbuf_tensor("cstD", [128, 3], F32))
        P0 = st.enter_context(nc.psum_tensor("P0", [128, 512], F32))
        P1 = st.enter_context(nc.psum_tensor("P1", [128, 512], F32))
        P2 = st.enter_context(nc.psum_tensor("P2", [128, 512], F32))
        P3 = st.enter_context(nc.psum_tensor("P3", [128, 512], F32))
        block = st.enter_context(nc.Block())
        sch = Sched(nc)

        def t_pre(eng, dsem):
            eng.dma_start(out=selk[:], in_=selk_t).then_inc(dsem, 16)
            return 1
        sch.task("sync", t_pre, writes=["selk"], dma=1, tag="preD")

        def t_cst(eng):
            eng.memset(cst[:, 0:1], -8.0)
            eng.memset(cst[:, 1:2], -10.0)
            return eng.memset(cst[:, 2:3], 7.5)
        sch.task("vector", t_cst, writes=["cstD"], tag="cstD")

        jobs = [(gp_t, gsc_t, wgo_t, 16, "g"), (up_t, usc_t, wuo_t, 16, "u"),
                (dp_t, dsc_t, wdo_t, 16, "d")]
        for src_t, sc_t, out_t, nkt, jn in jobs:
            for kt in range(nkt):
                fx = f"{jn}{kt}"

                def t_ld(eng, dsem, src_t=src_t, sc_t=sc_t, kt=kt):
                    eng.dma_start(out=pk[:], in_=src_t[kt * 128:(kt + 1) * 128, :]).then_inc(dsem, 16)
                    eng.dma_start(out=scrow[:], in_=sc_t[kt * 4:(kt + 1) * 4, :]).then_inc(dsem, 16)
                    return 2
                sch.task("gpsimd", t_ld, writes=["pk", "scrow"], dma=2, tag="ld" + fx)

                def t_scmm(eng):
                    for chn, pb in ((0, P0), (1, P1), (2, P2), (3, P3)):
                        i = eng.matmul(pb[:, 0:512], selk[:], scrow[:, chn * 512:(chn + 1) * 512],
                                       start=True, stop=True)
                    return i
                sch.task("tensor", t_scmm, reads=["selk", "scrow"],
                         writes=["P0", "P1", "P2", "P3"], tag="scmm" + fx)

                def t_scev(eng):
                    for chn, pb in ((0, P0), (1, P1), (2, P2), (3, P3)):
                        i = eng.tensor_copy(out=scb[:, chn * 512:(chn + 1) * 512], in_=pb[:, 0:512])
                    return i
                sch.task("vector", t_scev, reads=["P0", "P1", "P2", "P3"], writes=["scb"],
                         tag="scev" + fx)

                for j in range(8):
                    jfx = fx + f"j{j}"

                    def t_nib(eng, j=j):
                        eng.tensor_scalar(out=nib[:], in0=pk[:], scalar1=4 * j, scalar2=15,
                                          op0=ALU.logical_shift_right, op1=ALU.bitwise_and)
                        return eng.tensor_copy(out=nf[:, 0:256], in_=nib[:])
                    sch.task("vector", t_nib, reads=["pk"], writes=["nib", "nf"], tag="nb" + jfx)

                    def t_sgn(eng):
                        return eng.activation(out=sgn[:, 0:256], in_=nf[:, 0:256], func=AF.Sign,
                                              scale=-1.0, bias=cst[:, 2:3])
                    sch.task("scalar", t_sgn, reads=["nf"], writes=["sgn"], tag="sg" + jfx)

                    def t_mp(eng):
                        return eng.scalar_tensor_tensor(out=mp[:, 0:256], in0=sgn[:, 0:256],
                                                        scalar=4.0, in1=nf[:, 0:256],
                                                        op0=ALU.mult, op1=ALU.add)
                    sch.task("vector", t_mp, reads=["sgn", "nf"], writes=["mp"], tag="mp" + jfx)

                    def t_r46(eng):
                        eng.activation(out=r4[:, 0:256], in_=mp[:, 0:256], func=AF.Relu,
                                       bias=cst[:, 0:1])
                        return eng.activation(out=r6[:, 0:256], in_=mp[:, 0:256], func=AF.Relu,
                                              bias=cst[:, 1:2])
                    sch.task("scalar", t_r46, reads=["mp", "cstD"], writes=["r4", "r6"],
                             tag="r46" + jfx)

                    def t_val(eng, j=j):
                        eng.scalar_tensor_tensor(out=r6[:, 0:256], in0=mp[:, 0:256], scalar=0.5,
                                                 in1=r6[:, 0:256], op0=ALU.mult, op1=ALU.add)
                        eng.scalar_tensor_tensor(out=r6[:, 0:256], in0=r4[:, 0:256], scalar=0.5,
                                                 in1=r6[:, 0:256], op0=ALU.mult, op1=ALU.add)
                        eng.tensor_scalar_add(out=r6[:, 0:256], in0=r6[:, 0:256], scalar1=-2.0)
                        eng.tensor_mul(out=r6[:, 0:256], in0=r6[:, 0:256], in1=sgn[:, 0:256])
                        return eng.tensor_mul(
                            out=wout[:].rearrange("p (a b) -> p a b", b=8)[:, :, j:j + 1],
                            in0=r6[:, 0:256].rearrange("p (a b) -> p a b", b=1),
                            in1=scb[:].rearrange("p (a b) -> p a b", b=8)[:, :, j:j + 1])
                    sch.task("vector", t_val, reads=["mp", "r4", "r6", "sgn", "scb"],
                             writes=["wout", "r6"], tag="vl" + jfx)

                def t_st(eng, dsem, out_t=out_t, kt=kt):
                    eng.dma_start(out=out_t[kt * 128:(kt + 1) * 128, :], in_=wout[:]).then_inc(dsem, 16)
                    return 1
                sch.task("sync", t_st, reads=["wout"], dma=1, tag="st" + fx)

        @block.sync
        def _(eng):
            sch.run("sync", eng)

        @block.gpsimd
        def _(eng):
            sch.run("gpsimd", eng)

        @block.tensor
        def _(eng):
            sch.run("tensor", eng)

        @block.vector
        def _(eng):
            sch.run("vector", eng)

        @block.scalar
        def _(eng):
            sch.run("scalar", eng)
    return nc


def run_launch_D(inp):
    per_core = stage_launch_D(inp)
    nc = build_launch_D()
    res = run_bass_kernel_spmd(nc, per_core, core_ids=list(range(NCORE)))
    Wg = np.concatenate([res.results[c]["wgo"] for c in range(NCORE)], axis=1)
    Wu = np.concatenate([res.results[c]["wuo"] for c in range(NCORE)], axis=1)
    Wd = np.concatenate([res.results[c]["wdo"] for c in range(NCORE)], axis=0)
    return Wg, Wu, Wd


def stage_launch_E(inp, x3, Wg, Wu, Wd):
    T = 1024
    per_core = []
    for c in range(NCORE):
        sl = slice(c * T, (c + 1) * T)
        per_core.append({
            "xs": _f32(x3[:, sl]),
            "wg": Wg, "wu": Wu, "wd": Wd,
        })
    return per_core


def build_launch_E():
    nc = bass.Bass("TRN2", target_bir_lowering=False)
    T = 1024
    TH = 512
    NIT = 128
    xs_t = nc.dram_tensor("xs", [D, T], F32, kind="ExternalInput").ap()
    wg_t = nc.dram_tensor("wg", [D, 16384], BF16, kind="ExternalInput").ap()
    wu_t = nc.dram_tensor("wu", [D, 16384], BF16, kind="ExternalInput").ap()
    wd_t = nc.dram_tensor("wd", [16384, D], BF16, kind="ExternalInput").ap()
    out_t = nc.dram_tensor("out", [D, T], F32, kind="ExternalOutput").ap()

    from contextlib import ExitStack
    with ExitStack() as st:
        xb = st.enter_context(nc.sbuf_tensor("xb", [128, DT * TH], BF16))
        h4b = st.enter_context(nc.sbuf_tensor("h4b", [128, DT * TH], BF16))
        big = st.enter_context(nc.sbuf_tensor("big", [128, NIT * TH], BF16))
        sqb = big
        ffo = h4b
        tmp1 = st.enter_context(nc.sbuf_tensor("tmp1", [128, TH], BF16))
        tmp2 = st.enter_context(nc.sbuf_tensor("tmp2", [128, TH], F32))
        stg0 = st.enter_context(nc.sbuf_tensor("stg0", [128, TH], F32))
        stg1 = st.enter_context(nc.sbuf_tensor("stg1", [128, TH], F32))
        wb0 = st.enter_context(nc.sbuf_tensor("wb0", [128, 32 * 128], BF16))
        wb1 = st.enter_context(nc.sbuf_tensor("wb1", [128, 32 * 128], BF16))
        wdb0 = st.enter_context(nc.sbuf_tensor("wdb0", [128, 32 * 128], BF16))
        wdb1 = st.enter_context(nc.sbuf_tensor("wdb1", [128, 32 * 128], BF16))
        P0 = st.enter_context(nc.psum_tensor("P0", [128, 512], F32))
        P1 = st.enter_context(nc.psum_tensor("P1", [128, 512], F32))
        P2 = st.enter_context(nc.psum_tensor("P2", [128, 512], F32))
        P3 = st.enter_context(nc.psum_tensor("P3", [128, 512], F32))
        P4 = st.enter_context(nc.psum_tensor("P4", [128, 512], F32))
        P5 = st.enter_context(nc.psum_tensor("P5", [128, 512], F32))
        block = st.enter_context(nc.Block())
        sch = Sched(nc)
        ctx = Ctx(nc, sch, st)
        ctx.init_tasks()
        PS = [P0, P1, P2, P3]
        psn = ["P0", "P1", "P2", "P3"]
        WB = [(wb0, "wb0"), (wb1, "wb1")]
        WDB = [(wdb0, "wdb0"), (wdb1, "wdb1")]

        for th in range(2):
            fx = f"h{th}"

            def t_pre(eng, dsem, th=th):
                eng.dma_start(out=xb[:].rearrange("p (t n) -> p t n", t=DT),
                              in_=xs_t[:, th * TH:(th + 1) * TH]
                              .rearrange("(t p) n -> p t n", p=128)).then_inc(dsem, 16)
                return 1
            sch.task("gpsimd", t_pre, writes=["xb"], dma=1, tag="preE" + fx)

            xt = lambda dt: xb[:, dt * TH:(dt + 1) * TH]

            ctx.stats_pipeline(xt, "xb", DT, TH, sqb, "big", P4, "P4", P5, "P5",
                               "lnx3" + fx, use_mean=True)

            def t_h4(eng, xt=xt):
                for dt in range(DT):
                    eng.tensor_mul(out=tmp1[:], in0=xt(dt), in1=P5[:, 0:TH])
                    i = eng.tensor_add(out=h4b[:, dt * TH:(dt + 1) * TH],
                                       in0=tmp1[:], in1=P4[:, 0:TH])
                return i
            sch.task("vector", t_h4, reads=["xb", "P4", "P5"], writes=["h4b", "tmp1"],
                     tag="h4" + fx)

            for it in range(NIT):
                wsb, wn = WB[it % 2]
                def t_ldw(eng, dsem, it=it, wsb=wsb):
                    eng.dma_start(out=wsb[:, 0:16 * 128].rearrange("p (d n) -> p d n", d=16),
                                  in_=wg_t[:, it * 128:(it + 1) * 128]
                                  .rearrange("(d p) n -> p d n", p=128)).then_inc(dsem, 16)
                    eng.dma_start(out=wsb[:, 16 * 128:32 * 128].rearrange("p (d n) -> p d n", d=16),
                                  in_=wu_t[:, it * 128:(it + 1) * 128]
                                  .rearrange("(d p) n -> p d n", p=128)).then_inc(dsem, 16)
                    return 2
                sch.task("gpsimd", t_ldw, writes=[wn], dma=2, tag=f"ldw{it}" + fx)

                def t_mm(eng, wsb=wsb):
                    for dt in range(DT):
                        eng.matmul(P0[:, 0:TH], wsb[:, dt * 128:(dt + 1) * 128],
                                   h4b[:, dt * TH:(dt + 1) * TH],
                                   start=(dt == 0), stop=(dt == DT - 1))
                    for dt in range(DT):
                        i = eng.matmul(P1[:, 0:TH], wsb[:, 2048 + dt * 128:2048 + (dt + 1) * 128],
                                       h4b[:, dt * TH:(dt + 1) * TH],
                                       start=(dt == 0), stop=(dt == DT - 1))
                    return i
                sch.task("tensor", t_mm, reads=[wn, "h4b"], writes=["P0", "P1"],
                         tag=f"mm{it}" + fx)

                def t_g1(eng):
                    return eng.activation(out=tmp2[:], in_=P0[:, 0:TH], func=AF.Square)
                sch.task("scalar", t_g1, reads=["P0"], writes=["tmp2"], tag=f"g1{it}" + fx)

                def t_g2(eng):
                    eng.tensor_scalar(out=tmp2[:], in0=tmp2[:], scalar1=0.044715, scalar2=1.0,
                                      op0=ALU.mult, op1=ALU.add)
                    return eng.tensor_mul(out=tmp2[:], in0=tmp2[:], in1=P0[:, 0:TH])
                sch.task("vector", t_g2, reads=["tmp2", "P0"], writes=["tmp2"], tag=f"g2{it}" + fx)

                def t_g3(eng):
                    return eng.activation(out=tmp2[:], in_=tmp2[:], func=AF.Tanh,
                                          scale=0.7978845608028654)
                sch.task("scalar", t_g3, reads=["tmp2"], writes=["tmp2"], tag=f"g3{it}" + fx)

                def t_g4(eng, it=it):
                    eng.tensor_scalar_add(out=tmp2[:], in0=tmp2[:], scalar1=1.0)
                    eng.tensor_mul(out=tmp2[:], in0=tmp2[:], in1=P0[:, 0:TH])
                    eng.tensor_scalar_mul(out=tmp2[:], in0=tmp2[:], scalar1=0.5)
                    return eng.tensor_mul(out=big[:, it * TH:(it + 1) * TH],
                                          in0=tmp2[:], in1=P1[:, 0:TH])
                sch.task("vector", t_g4, reads=["tmp2", "P0", "P1"], writes=["big", "tmp2"],
                         tag=f"g4{it}" + fx)

            for nt in range(16):
                pb, pn = PS[nt % 4], psn[nt % 4]
                for kq in range(4):
                    wsb, wn = WDB[kq % 2]
                    def t_ldw(eng, dsem, nt=nt, kq=kq, wsb=wsb):
                        eng.dma_start(out=wsb[:].rearrange("p (d n) -> p d n", d=32),
                                      in_=wd_t[kq * 4096:(kq + 1) * 4096, nt * 128:(nt + 1) * 128]
                                      .rearrange("(d p) n -> p d n", p=128)).then_inc(dsem, 16)
                        return 1
                    sch.task("gpsimd", t_ldw, writes=[wn], dma=1, tag=f"lddn{nt}_{kq}" + fx)

                    def t_mmd(eng, pb=pb, kq=kq, wsb=wsb):
                        for it in range(32):
                            i = eng.matmul(pb[:, 0:TH], wsb[:, it * 128:(it + 1) * 128],
                                           big[:, (kq * 32 + it) * TH:(kq * 32 + it + 1) * TH],
                                           start=(kq == 0 and it == 0),
                                           stop=(kq == 3 and it == 31))
                        return i
                    sch.task("tensor", t_mmd, reads=[wn, "big"], writes=[pn],
                             tag=f"mmdn{nt}_{kq}" + fx)

                def t_ev(eng, nt=nt, pb=pb):
                    return eng.tensor_copy(out=ffo[:, nt * TH:(nt + 1) * TH], in_=pb[:, 0:TH])
                sch.task("vector", t_ev, reads=[pn], writes=["ffo"], tag=f"evdn{nt}" + fx)

            ft = lambda dt: ffo[:, dt * TH:(dt + 1) * TH]
            ctx.stats_pipeline(ft, "ffo", DT, TH, sqb, "big", P4, "P4", P5, "P5",
                               "lnf" + fx, use_mean=True)

            for dt in range(DT):
                stg, sn = (stg0, "stg0") if dt % 2 == 0 else (stg1, "stg1")
                def t_out(eng, dt=dt, stg=stg, xt=xt, ft=ft):
                    eng.tensor_mul(out=tmp1[:], in0=ft(dt), in1=P5[:, 0:TH])
                    eng.tensor_add(out=tmp1[:], in0=tmp1[:], in1=P4[:, 0:TH])
                    return eng.tensor_add(out=stg[:], in0=xt(dt), in1=tmp1[:])
                sch.task("vector", t_out, reads=["ffo", "P4", "P5", "xb"],
                         writes=[sn, "tmp1"], tag=f"out{dt}" + fx)

                def t_st(eng, dsem, dt=dt, stg=stg, th=th):
                    eng.dma_start(out=out_t[dt * 128:(dt + 1) * 128, th * TH:(th + 1) * TH],
                                  in_=stg[:]).then_inc(dsem, 16)
                    return 1
                sch.task("sync", t_st, reads=[sn], dma=1, tag=f"st{dt}" + fx)

        @block.sync
        def _(eng):
            sch.run("sync", eng)

        @block.gpsimd
        def _(eng):
            sch.run("gpsimd", eng)

        @block.tensor
        def _(eng):
            sch.run("tensor", eng)

        @block.vector
        def _(eng):
            sch.run("vector", eng)

        @block.scalar
        def _(eng):
            sch.run("scalar", eng)
    return nc


def run_launch_E(inp, x3, Wg, Wu, Wd):
    per_core = stage_launch_E(inp, x3, Wg, Wu, Wd)
    nc = build_launch_E()
    res = run_bass_kernel_spmd(nc, per_core, core_ids=list(range(NCORE)))
    return np.concatenate([res.results[c]["out"] for c in range(NCORE)], axis=1)


def kernel(**inputs):
    gatedT = run_launch_A(inputs)
    x2 = run_launch_B(inputs, gatedT)
    x3 = run_launch_C(inputs, x2)
    Wg, Wu, Wd = run_launch_D(inputs)
    outT = run_launch_E(inputs, x3, Wg, Wu, Wd)
    return np.ascontiguousarray(outT.T)[None]
